# version 16
# baseline (speedup 1.0000x reference)
import contextlib
import numpy as np
import ml_dtypes
import concourse.bass as bass
import concourse.mybir as mybir
from concourse.bass_utils import run_bass_kernel_spmd

F32 = mybir.dt.float32
BF16 = mybir.dt.bfloat16
U8 = mybir.dt.uint8
AF = mybir.ActivationFunctionType
ALU = mybir.AluOpType
AX = mybir.AxisListType

B, T, D = 8, 2048, 2048
NH, QL, KVL, NOPE, ROPE, VH, QKH = 16, 512, 512, 128, 64, 128, 192
HID = 8192
NTT = T // 128
DC = D // 128
NQ = 4
HC = HID // 128
NSTRIP = 4
SM_SCALE = 1.0 / float(np.sqrt(128.0))
LN_EPS = 1e-5
RMS_EPS = float(np.finfo(np.float32).eps)
THETA = 1e6

ENGS = ("t", "s", "v", "g", "sy")
DRAIN_ENGS = ("s", "v", "g")


class Ev:
    __slots__ = ("eng", "idx", "rec")

    def __init__(self, eng, idx, rec):
        self.eng, self.idx, self.rec = eng, idx, rec


class Rec:
    __slots__ = ("fn", "after", "dma", "slot", "needed", "sem_val", "ev")

    def __init__(self, fn, after, dma, slot):
        self.fn, self.after, self.dma = fn, list(after), dma
        self.slot = slot
        self.needed = False
        self.sem_val = None
        self.ev = None


class Prog:

    def __init__(self):
        self.recs = {e: [] for e in ENGS}

    def add(self, eng, fn, after=(), dma=False, slot=None):
        assert (slot is not None) == dma, "DMA records need a slot"
        flat = []
        for a in after:
            if a is None:
                continue
            if isinstance(a, (list, tuple)):
                flat.extend(x for x in a if x is not None)
            else:
                flat.append(a)
        r = Rec(fn, flat, dma, slot)
        lst = self.recs[eng]
        ev = Ev(eng, len(lst), r)
        r.ev = ev
        lst.append(r)
        return ev

    def t(self, fn, after=()):
        return self.add("t", fn, after)

    def s(self, fn, after=()):
        return self.add("s", fn, after)

    def v(self, fn, after=()):
        return self.add("v", fn, after)

    def g(self, fn, after=()):
        return self.add("g", fn, after)

    def gdma(self, fn, after=(), slot=None):
        return self.add("g", fn, after, dma=True, slot=slot)

    def sydma(self, fn, after=(), slot=None):
        return self.add("sy", fn, after, dma=True, slot=slot)

    def vdma(self, fn, after=(), slot=None):
        return self.add("v", fn, after, dma=True, slot=slot)

    def resolve(self):
        for eng in ENGS:
            for r in self.recs[eng]:
                if r.dma:
                    r.needed = True
                for ev in r.after:
                    ev.rec.needed = True
        self.slot_final = {}
        for eng in ENGS:
            cnt = {}
            for r in self.recs[eng]:
                if r.needed:
                    key = ("d", eng, r.slot) if r.dma else ("c", eng)
                    cnt[key] = cnt.get(key, 0) + (16 if r.dma else 1)
                    r.sem_val = cnt[key]
            for key, val in cnt.items():
                if key[0] == "d":
                    self.slot_final[key] = val

    def check(self):
        ptr = {e: 0 for e in ENGS}
        done = set()
        progress = True
        while progress:
            progress = False
            for e in ENGS:
                while ptr[e] < len(self.recs[e]):
                    r = self.recs[e][ptr[e]]
                    if all((ev.eng, ev.idx) in done for ev in r.after):
                        done.add((e, ptr[e]))
                        ptr[e] += 1
                        progress = True
                    else:
                        break
        stuck = {e: ptr[e] for e in ENGS if ptr[e] < len(self.recs[e])}
        if stuck:
            msgs = []
            for e, i in stuck.items():
                r = self.recs[e][i]
                blocked = [(ev.eng, ev.idx) for ev in r.after
                           if (ev.eng, ev.idx) not in done]
                msgs.append(f"{e}:{i} waits on {blocked}")
            raise RuntimeError("deadlock in recorded program: " + "; ".join(msgs))

    def emit(self, eng, eng_obj, sems):
        done = {}
        for i, r in enumerate(self.recs[eng]):
            pend = {}
            for ev in r.after:
                if ev.rec.dma:
                    key = ("d", ev.eng, ev.rec.slot)
                    if ev.rec.slot.startswith("setup"):
                        val = self.slot_final[key]
                    else:
                        val = ev.rec.sem_val
                else:
                    key = ("c", ev.eng)
                    val = ev.rec.sem_val
                assert val is not None, f"unsignalled dep {ev.eng}:{ev.idx}"
                if pend.get(key, 0) < val:
                    pend[key] = val
            for key in sorted(pend):
                val = pend[key]
                if done.get(key, 0) < val:
                    eng_obj.wait_ge(sems[key], val)
                    done[key] = val
            inst = r.fn(eng_obj)
            if r.needed:
                key = ("d", eng, r.slot) if r.dma else ("c", eng)
                inst.then_inc(sems[key], 16 if r.dma else 1)


def _permutations():
    nope = [h * QKH + j for h in range(NH) for j in range(NOPE)]
    pe_r = [h * QKH + NOPE + 2 * f for h in range(NH) for f in range(ROPE // 2)]
    pe_i = [h * QKH + NOPE + 2 * f + 1 for h in range(NH) for f in range(ROPE // 2)]
    qb_perm = np.array(nope + pe_r + pe_i)
    knope = [h * (NOPE + VH) + j for h in range(NH) for j in range(NOPE)]
    kv = [h * (NOPE + VH) + NOPE + j for h in range(NH) for j in range(VH)]
    kvb_perm = np.array(knope + kv)
    kva_perm = np.array(
        list(range(KVL))
        + [KVL + 2 * f for f in range(ROPE // 2)]
        + [KVL + 2 * f + 1 for f in range(ROPE // 2)]
    )
    return qb_perm, kvb_perm, kva_perm


class Ctx:
    pass


def build_program(upto="all", dbg=(), ncc=8):
    nc = bass.Bass("TRN2", target_bir_lowering=False)
    c = Ctx()
    c.nc = nc
    c.p = Prog()
    c.ncc = ncc
    c.dbg = set(dbg)
    stack = contextlib.ExitStack()
    c.stack = stack

    c.x = nc.dram_tensor("x", [T, D], F32, kind="ExternalInput").ap()
    c.wq_a = nc.dram_tensor("wq_a", [D, QL], F32, kind="ExternalInput").ap()
    c.wq_b = nc.dram_tensor("wq_b", [QL, NH * QKH], F32, kind="ExternalInput").ap()
    c.wkv_a = nc.dram_tensor("wkv_a", [D, KVL + ROPE], F32, kind="ExternalInput").ap()
    c.wkv_b = nc.dram_tensor("wkv_b", [KVL, NH * (NOPE + VH)], F32,
                             kind="ExternalInput").ap()
    c.wo = nc.dram_tensor("wo", [D, D], F32, kind="ExternalInput").ap()
    c.wsl1 = nc.dram_tensor("wsl1", [1024, D], F32, kind="ExternalInput").ap()
    c.wsl2 = nc.dram_tensor("wsl2", [1024, D], F32, kind="ExternalInput").ap()
    c.b_qa = nc.dram_tensor("b_qa", [QL], F32, kind="ExternalInput").ap()
    c.b_qb = nc.dram_tensor("b_qb", [NH * QKH], F32, kind="ExternalInput").ap()
    c.b_kva = nc.dram_tensor("b_kva", [KVL + ROPE], F32, kind="ExternalInput").ap()
    c.b_kvb = nc.dram_tensor("b_kvb", [NH * (NOPE + VH)], F32,
                             kind="ExternalInput").ap()
    c.b_wo = nc.dram_tensor("b_wo", [D], F32, kind="ExternalInput").ap()
    c.id_f32 = nc.dram_tensor("id_f32", [128, 128], F32, kind="ExternalInput").ap()
    c.id_bf = nc.dram_tensor("id_bf", [128, 128], F32, kind="ExternalInput").ap()
    c.ones_col = nc.dram_tensor("ones_col", [128, 1], F32, kind="ExternalInput").ap()
    c.ones_row = nc.dram_tensor("ones_row", [1, 128], F32, kind="ExternalInput").ap()
    c.rep32 = nc.dram_tensor("rep32", [32, 128], F32, kind="ExternalInput").ap()
    c.out = nc.dram_tensor("out", [T, D], F32, kind="ExternalOutput").ap()

    c.q_scr = nc.dram_tensor("q_scr", [NQ, NH, 192, 512], BF16, kind="Internal").ap()
    c.v_scr = nc.dram_tensor("v_scr", [NH, T, VH], BF16, kind="Internal").ap()
    c.x2_scr = nc.dram_tensor("x2_scr", [NTT, 128, D], F32, kind="Internal").ap()
    c.o1_scr = nc.dram_tensor("o1_scr", [NTT, 128, D], F32, kind="Internal").ap()
    c.gin1 = nc.dram_tensor("gin1", [8, 128, D], BF16, kind="Internal").ap()
    c.gin2 = nc.dram_tensor("gin2", [16, 128, 1024], BF16, kind="Internal").ap()
    c.gout1 = nc.dram_tensor("gout1", [HC * 128, D], BF16, kind="Internal").ap()
    c.gout2 = nc.dram_tensor("gout2", [16384, 1024], BF16, kind="Internal").ap()

    c.biga = stack.enter_context(nc.sbuf_tensor("biga", [128, 65536], U8))
    c.bigb = stack.enter_context(nc.sbuf_tensor("bigb", [128, 65536], U8))
    c.pool = stack.enter_context(nc.sbuf_tensor("pool", [128, 65536], U8))
    c.cons = stack.enter_context(nc.sbuf_tensor("cons", [128, 13312], U8))

    c.pa = [stack.enter_context(nc.psum_tensor(f"pa{i}", [128, 1024], F32))
            for i in range(4)]

    def bank(k):
        return c.pa[k // 2][:, (k % 2) * 512:(k % 2) * 512 + 512]

    def bank_bf(k):
        return c.pa[k // 2][:].bitcast(BF16)[:, (k % 2) * 1024:(k % 2) * 1024 + 1024]

    c.bank, c.bank_bf = bank, bank_bf


    def view(arena, off, parts, free, dt):
        nb = free * mybir.dt.size(dt)
        assert off % 64 == 0 and off + nb <= arena.shape[1], (off, nb)
        return arena[0:parts, off:off + nb].bitcast(dt)

    c.view = view

    _build_consts(c)
    _p0(c)
    if upto == "p0":
        _finish(c)
        return nc, c
    _p1(c)
    if upto == "p1":
        _finish(c)
        return nc, c
    _p2a(c)
    if upto == "p2a":
        _finish(c)
        return nc, c
    _p2b(c)
    if upto == "p2b":
        _finish(c)
        return nc, c
    _p3(c)
    if upto == "p3":
        _finish(c)
        return nc, c
    _p4(c)
    if upto == "p4":
        _finish(c)
        return nc, c
    _p56(c)
    if upto == "p56":
        _finish(c)
        return nc, c
    _p8(c)
    _finish(c)
    return nc, c


def _build_consts(c):
    v = c.view
    off = 0

    def alloc(free, dt, parts=128):
        nonlocal off
        tt = v(c.cons, off, parts, free, dt)
        off += ((free * mybir.dt.size(dt) + 63) // 64) * 64
        return tt

    c.sb_idf = alloc(128, F32)
    c.sb_idb = alloc(128, BF16)
    c.sb_ones_col = alloc(1, BF16)
    c.sb_ones_row = alloc(128, F32, parts=1)
    c.sb_rep32 = alloc(128, F32, parts=32)
    c.sb_b_qa = alloc(4, F32)
    c.sb_b_qb = alloc(24, F32)
    c.sb_b_kva = alloc(4, F32)
    c.sb_b_kpe = alloc(1, F32, parts=64)
    c.sb_b_kvb = alloc(32, F32)
    c.sb_wob_bc = alloc(2048, F32)
    c.sm = []
    for i in range(4):
        s = Ctx()
        for nm in ("sum", "sq", "mean", "tmp", "var", "log", "rstd", "bias"):
            setattr(s, nm, alloc(1, F32))
        c.sm.append(s)
    assert off <= c.cons.shape[1], off


def _ln_small_chain(c, s, n_inv, eps, e_sum, e_sq):
    p = c.p
    e1 = p.v(lambda e: e.tensor_scalar(s.mean, s.sum, n_inv, None, ALU.mult),
             after=[e_sum])
    e2 = p.v(lambda e: e.scalar_tensor_tensor(s.tmp, s.mean, -1.0, s.mean,
                                              ALU.mult, ALU.mult), after=[e1])
    e3 = p.v(lambda e: e.scalar_tensor_tensor(s.var, s.sq, n_inv, s.tmp,
                                              ALU.mult, ALU.add), after=[e2, e_sq])
    e4 = p.v(lambda e: e.tensor_scalar(s.var, s.var, eps, None, ALU.add),
             after=[e3])
    e5 = p.s(lambda e: e.activation(s.log, s.var, AF.Ln), after=[e4])
    e6 = p.s(lambda e: e.activation(s.rstd, s.log, AF.Exp, scale=-0.5),
             after=[e5])
    e7 = p.v(lambda e: e.scalar_tensor_tensor(s.bias, s.mean, -1.0, s.rstd,
                                              ALU.mult, ALU.mult),
             after=[e2, e6])
    return e6, e7


def _ln_stats_apply(c, src_ap, dst_ap, n_inv, eps, setidx, after, scr_s, scr_v,
                    apply_after=()):
    p = c.p
    s = c.sm[setidx]
    e_sum = p.s(lambda e: e.activation(scr_s, src_ap, AF.Identity,
                                       accum_out=s.sum), after=after)
    e_sq = p.v(lambda e: e.scalar_tensor_tensor(scr_v, src_ap, 1.0, src_ap,
                                                ALU.mult, ALU.mult,
                                                accum_out=s.sq), after=after)
    e_rstd, e_bias = _ln_small_chain(c, s, n_inv, eps, e_sum, e_sq)
    e8 = p.s(lambda e: e.activation(dst_ap, src_ap, AF.Identity, bias=s.bias,
                                    scale=s.rstd),
             after=[e_bias, e_rstd] + list(apply_after))
    return e8


def _transpose16(c, src_tile, dst_fn, dt, after, banks, evac_eng="s"):
    p = c.p
    ident = c.sb_idb if dt == BF16 else c.sb_idf
    per_bank = 8 if dt == BF16 else 4
    ngroups = 16 // per_bank
    group_last = {}
    for j in range(16):
        b = banks[j // per_bank]
        if dt == BF16:
            out_ap = c.bank_bf(b)[:, (j % per_bank) * 128:(j % per_bank) * 128 + 128]
        else:
            out_ap = c.bank(b)[:, (j % per_bank) * 128:(j % per_bank) * 128 + 128]
        src = src_tile[:, j * 128:j * 128 + 128]
        ev = p.t(lambda e, o=out_ap, si=src: e.transpose(o, si, ident),
                 after=(after if j == 0 else []))
        group_last[j // per_bank] = ev
    evs = []
    for gi in range(ngroups):
        b = banks[gi]
        src_ps = (c.bank_bf(b) if dt == BF16 else c.bank(b)).rearrange(
            "p (n j) -> p n j", j=128)
        if evac_eng == "s":
            ev = p.s(lambda e, d=dst_fn(gi), sp=src_ps: e.copy(d, sp),
                     after=[group_last[gi]])
        else:
            ev = p.v(lambda e, d=dst_fn(gi), sp=src_ps: e.tensor_copy(d, sp),
                     after=[group_last[gi]])
        evs.append(ev)
    return evs


def _dbg_out(c, name, src_ap, shape, dt, after):
    tt = c.nc.dram_tensor(name, shape, dt, kind="ExternalOutput").ap()
    c.p.sydma(lambda e: e.dma_start(tt, src_ap), after=after, slot="dbg")


def _p0(c):
    p, v = c.p, c.view
    sy, g = p.sydma, p.gdma
    c.e_idf = sy(lambda e: e.dma_start(c.sb_idf, c.id_f32), slot="setup")
    c.e_idb = g(lambda e: e.dma_start(c.sb_idb, c.id_bf), slot="setupg")
    c.e_ones = g(lambda e: e.dma_start(c.sb_ones_col, c.ones_col), slot="setupg")
    c.e_onesr = sy(lambda e: e.dma_start(c.sb_ones_row, c.ones_row), slot="setup")
    c.e_rep = sy(lambda e: e.dma_start(c.sb_rep32, c.rep32), slot="setup")
    for (dst, src, n) in [(c.sb_b_qa, c.b_qa, 4), (c.sb_b_qb, c.b_qb, 24),
                          (c.sb_b_kva, c.b_kva, 4), (c.sb_b_kvb, c.b_kvb, 32)]:
        for k in range(n):
            sy(lambda e, d=dst[:, k:k + 1],
               s=src[k * 128:(k + 1) * 128].unsqueeze(1): e.dma_start(d, s),
               slot="setup")
    c.e_bias = sy(lambda e: e.dma_start(c.sb_b_kpe,
                                        c.b_kva[KVL:KVL + 64].unsqueeze(1)),
                  slot="setup")
    c.sb_cb = v(c.pool, 0, 128, 2048, F32)
    c.sb_sb2 = v(c.pool, 8192, 128, 2048, F32)
    tv = v(c.pool, 16384, 1, 2048, F32)
    iv = v(c.pool, 24576, 1, 32, F32)
    ifr = v(c.pool, 24832, 1, 32, F32)
    red = v(c.pool, 25088, 32, 2048, F32)
    red2 = v(c.pool, 33280, 32, 2048, F32)
    sin32 = v(c.pool, 41472, 32, 2048, F32)
    cos32 = v(c.pool, 49664, 32, 2048, F32)
    fr = v(c.bigb, 34816, 32, 2048, F32)
    ka = v(c.bigb, 43008, 32, 2048, F32)
    kf2 = v(c.bigb, 51200, 32, 2048, F32)

    e_iota = p.g(lambda e: e.iota(tv, [[1, 2048]], channel_multiplier=0,
                                  allow_small_or_imprecise_dtypes=True))
    e_iota2 = p.g(lambda e: e.iota(iv, [[1, 32]], channel_multiplier=0,
                                   allow_small_or_imprecise_dtypes=True),
                  after=[e_iota])
    e_ifr = p.s(lambda e: e.activation(ifr, iv, AF.Exp,
                                       scale=-float(np.log(THETA) / 32)),
                after=[e_iota2])
    e_frs = []
    for j in range(4):
        e_mm = p.t(lambda e, jj=j: e.matmul(
            c.bank(0)[0:32, :], ifr, tv[:, jj * 512:(jj + 1) * 512],
            start=True, stop=True, skip_group_check=True),
            after=[e_ifr] + (e_frs[-1:] if e_frs else []))
        e_cp = p.v(lambda e, jj=j: e.tensor_copy(
            fr[:, jj * 512:(jj + 1) * 512], c.bank(0)[0:32, :]), after=[e_mm])
        e_frs.append(e_cp)
    M = 8388608.0
    TWO_PI = float(2.0 * np.pi)
    e1 = p.v(lambda e: e.tensor_scalar(ka, fr, 1.0 / TWO_PI, M, ALU.mult,
                                       ALU.add), after=[e_frs[-1]])
    e2 = p.v(lambda e: e.tensor_scalar(kf2, ka, -M, None, ALU.add), after=[e1])
    e3 = p.v(lambda e: e.scalar_tensor_tensor(red, kf2, -TWO_PI, fr, ALU.mult,
                                              ALU.add), after=[e2])
    PI = float(np.pi)
    e3b = p.v(lambda e: e.tensor_scalar(red, red, PI, -PI, ALU.min, ALU.max),
              after=[e3])
    e_sin = p.s(lambda e: e.activation(sin32, red, AF.Sin), after=[e3b])
    e4 = p.v(lambda e: e.tensor_scalar(ka, fr, 1.0 / TWO_PI, 0.25, ALU.mult,
                                       ALU.add), after=[e3])
    e5 = p.v(lambda e: e.tensor_scalar(kf2, ka, M, -M, ALU.add, ALU.add),
             after=[e4])
    e6 = p.v(lambda e: e.scalar_tensor_tensor(red2, kf2, -1.0, ka, ALU.mult,
                                              ALU.add), after=[e5])
    e7 = p.v(lambda e: e.tensor_scalar(red2, red2, TWO_PI, None, ALU.mult),
             after=[e6])
    e7b = p.v(lambda e: e.tensor_scalar(red2, red2, PI, -PI, ALU.min, ALU.max),
              after=[e7])
    e_cos = p.s(lambda e: e.activation(cos32, red2, AF.Sin), after=[e7b, e_sin])
    cb_evs = []
    for (src, dst) in [(cos32, c.sb_cb), (sin32, c.sb_sb2)]:
        for j in range(4):
            e_mm = p.t(lambda e, ss=src, jj=j: e.matmul(
                c.bank(1), c.sb_rep32, ss[:, jj * 512:(jj + 1) * 512],
                start=True, stop=True, skip_group_check=True),
                after=[e_cos, c.e_rep] + (cb_evs[-1:] if cb_evs else []))
            e_cp = p.v(lambda e, dd=dst, jj=j: e.tensor_copy(
                dd[:, jj * 512:(jj + 1) * 512], c.bank(1)), after=[e_mm])
            cb_evs.append(e_cp)
    c.e_cs = cb_evs[-1]
    wob_row = v(c.pool, 25088, 1, 2048, F32)
    e_wr = sy(lambda e: e.dma_start(wob_row, c.b_wo.unsqueeze(0)), after=[e_sin],
              slot="wob")
    wob_evs = []
    for j in range(4):
        e_mm = p.t(lambda e, jj=j: e.matmul(
            c.bank(2), c.sb_ones_row, wob_row[:, jj * 512:(jj + 1) * 512],
            start=True, stop=True, skip_group_check=True),
            after=[e_wr, c.e_onesr] + (wob_evs[-1:] if wob_evs else []))
        e_cp = p.v(lambda e, jj=j: e.tensor_copy(
            c.sb_wob_bc[:, jj * 512:(jj + 1) * 512], c.bank(2)), after=[e_mm])
        wob_evs.append(e_cp)
    c.e_wob = wob_evs[-1]
    c.sb_wqa = v(c.bigb, 0, 128, 16 * 512, BF16)
    c.sb_wkva = v(c.bigb, 16384, 128, 16 * 576, BF16)
    c.e_wqa = g(lambda e: e.dma_start(
        c.sb_wqa.rearrange("p (cc m) -> p cc m", m=512),
        c.wq_a.rearrange("(cc p) m -> p cc m", p=128)), slot="setupg")
    c.e_wkva = g(lambda e: e.dma_start(
        c.sb_wkva.rearrange("p (cc m) -> p cc m", m=576),
        c.wkv_a.rearrange("(cc p) m -> p cc m", p=128)), slot="setupg")
    if "rope" in c.dbg:
        _dbg_out(c, "dbg_cb", c.sb_cb, [128, 2048], F32, [c.e_cs])
        _dbg_out(c, "dbg_sb", c.sb_sb2, [128, 2048], F32, [c.e_cs])
    c.e_p0_free = [c.e_cs, c.e_wob]


def _p1(c):
    p, v = c.p, c.view
    c.xnT = v(c.biga, 0, 128, 32768, BF16)
    c.xbuf = [v(c.pool, 16384 + i * 8192, 128, 2048, F32) for i in range(2)]
    c.xn_tok = [v(c.pool, 32768 + i * 4096, 128, 2048, BF16) for i in range(2)]
    c.scr_s = v(c.pool, 40960, 128, 2048, BF16)
    c.scr_v = v(c.pool, 45056, 128, 2048, BF16)
    prev_apply = [None, None]
    prev_evac = None
    xnT3 = c.xnT.rearrange("p (dc t) -> p dc t", t=2048)
    pend = []

    def emit_tp(tt, e_ap):
        nonlocal prev_evac

        def dst_fn(gi, t_=tt):
            return xnT3[:, gi * 8:gi * 8 + 8, t_ * 128:t_ * 128 + 128]
        evs = _transpose16(c, c.xn_tok[tt % 2], dst_fn, BF16,
                           [e_ap, c.e_idb, prev_evac], banks=(6, 7))
        prev_evac = evs[-1]
        return evs[-1]

    for tt in range(NTT):
        bi = tt % 2
        aft = [prev_apply[bi]]
        if tt == 0:
            aft += c.e_p0_free
        e_x = p.sydma(lambda e, t_=tt, b=bi: e.dma_start(
            c.xbuf[b], c.x[t_ * 128:(t_ + 1) * 128, :]), after=aft,
            slot=f"xb{bi}")
        e_ap = _ln_stats_apply(c, c.xbuf[bi], c.xn_tok[bi], 1.0 / D, LN_EPS,
                               bi, [e_x], c.scr_s, c.scr_v)
        prev_apply[bi] = e_ap
        pend.append((tt, e_ap))
        if len(pend) > 1:
            emit_tp(*pend.pop(0))
    while pend:
        emit_tp(*pend.pop(0))
    c.e_p1 = prev_evac
    if "p1" in c.dbg:
        _dbg_out(c, "dbg_xnT", c.xnT, [128, 32768], BF16, [c.e_p1])


def _p2a(c):
    p, v = c.p, c.view
    c.q_aT = v(c.pool, 16384, 128, 4 * 2048, BF16)
    c.ckvT = v(c.pool, 32768, 128, 4 * 2048, BF16)
    c.kpe_pre = v(c.pool, 49152, 64, 2048, F32)
    c.kpeT = v(c.pool, 61440, 64, 2048, BF16)
    km = [v(c.bigb, 34816 + i * 2048, 32, 512, F32) for i in range(4)]
    last_bank_user = {}

    def proj(dst, wsb, wstride, oc, ts, bk, bias_ap, after_w, dst_col):
        last_mm = None
        for dc in range(DC):
            lhs = wsb[:, dc * wstride + oc * 128: dc * wstride + oc * 128 + 128]
            rhs = c.xnT[:, dc * 2048 + ts * 512: dc * 2048 + ts * 512 + 512]
            aft = [] if dc else [after_w, c.e_p1, last_bank_user.get(bk)]
            last_mm = p.t(lambda e, l=lhs, r=rhs, d=dc: e.matmul(
                c.bank(bk), l, r, start=(d == 0), stop=(d == DC - 1),
                skip_group_check=True), after=aft)
        e_ev = p.s(lambda e, d_=dst[:, dst_col:dst_col + 512], b_=bias_ap,
                   k_=bk: e.activation(d_, c.bank(k_), AF.Identity, bias=b_,
                                       scale=1.0), after=[last_mm, c.e_bias])
        last_bank_user[bk] = e_ev
        return e_ev

    c.e_qa, c.e_ckv = [], []
    for ts in range(NQ):
        for oc in range(4):
            c.e_qa.append(proj(c.q_aT, c.sb_wqa, 512, oc, ts,
                               (ts % 2) * 4 + oc, c.sb_b_qa[:, oc:oc + 1],
                               c.e_wqa, oc * 2048 + ts * 512))
    for ts in range(NQ):
        for oc in range(4):
            c.e_ckv.append(proj(c.ckvT, c.sb_wkva, 576, oc, ts,
                                (ts % 2) * 4 + oc, c.sb_b_kva[:, oc:oc + 1],
                                c.e_wkva, oc * 2048 + ts * 512))
    c.e_kpe = []
    for ts in range(NQ):
        last_mm = None
        for dc in range(DC):
            lhs = c.sb_wkva[:, dc * 576 + 512: dc * 576 + 576]
            rhs = c.xnT[:, dc * 2048 + ts * 512: dc * 2048 + ts * 512 + 512]
            aft = [] if dc else [c.e_wkva, c.e_p1, last_bank_user.get(0)]
            last_mm = p.t(lambda e, l=lhs, r=rhs, d=dc: e.matmul(
                c.bank(0)[0:64, :], l, r, start=(d == 0), stop=(d == DC - 1),
                skip_group_check=True), after=aft)
        e_ev = p.s(lambda e, t_=ts: e.activation(
            c.kpe_pre[:, t_ * 512:(t_ + 1) * 512], c.bank(0)[0:64, :],
            AF.Identity, bias=c.sb_b_kpe, scale=1.0),
            after=[last_mm, c.e_bias])
        last_bank_user[0] = e_ev
        sl = slice(ts * 512, ts * 512 + 512)
        kr, ki = c.kpe_pre[0:32, sl], c.kpe_pre[32:64, sl]
        cbs, sbs = c.sb_cb[0:32, sl], c.sb_sb2[0:32, sl]
        cbs32, sbs32 = c.sb_cb[32:64, sl], c.sb_sb2[32:64, sl]
        prev_o = c.e_kpe[-2:] if c.e_kpe else []
        m0 = p.v(lambda e, a_=kr, b_=cbs: e.tensor_tensor(km[0], a_, b_,
                                                          ALU.mult),
                 after=[e_ev, c.e_cs] + prev_o)
        m1 = p.v(lambda e, a_=ki, b_=sbs32: e.tensor_tensor(km[1], a_, b_,
                                                            ALU.mult),
                 after=[e_ev] + prev_o)
        m2 = p.v(lambda e, a_=kr, b_=sbs: e.tensor_tensor(km[2], a_, b_,
                                                          ALU.mult),
                 after=[e_ev] + prev_o)
        m3 = p.v(lambda e, a_=ki, b_=cbs32: e.tensor_tensor(km[3], a_, b_,
                                                            ALU.mult),
                 after=[e_ev] + prev_o)
        o0 = p.v(lambda e, d_=c.kpeT[0:32, sl]: e.tensor_tensor(
            d_, km[0], km[1], ALU.subtract), after=[m0, m1])
        o1 = p.v(lambda e, d_=c.kpeT[32:64, sl]: e.tensor_tensor(
            d_, km[2], km[3], ALU.add), after=[m2, m3])
        c.e_kpe += [o0, o1]
    if "p2a" in c.dbg:
        _dbg_out(c, "dbg_kpe_pre", c.kpe_pre, [64, 2048], F32, c.e_kpe[-2:])
        _dbg_out(c, "dbg_qaT", c.q_aT, [128, 8192], BF16, c.e_qa[-1:])
        _dbg_out(c, "dbg_ckvT", c.ckvT, [128, 8192], BF16, c.e_ckv[-1:])
        _dbg_out(c, "dbg_kpeT", c.kpeT, [64, 2048], BF16, c.e_kpe[-2:])


def _p2b(c):
    p, v = c.p, c.view
    c.sb_wqb = v(c.bigb, 0, 128, 4 * 3072, BF16)
    c.sb_wkvb = v(c.bigb, 24576, 128, 4 * 4096, BF16)
    e_wqb = p.gdma(lambda e: e.dma_start(
        c.sb_wqb.rearrange("p (cc m) -> p cc m", m=3072),
        c.wq_b.rearrange("(cc p) m -> p cc m", p=128)),
        after=[c.e_qa[-1], c.e_ckv[-1]] + c.e_kpe[-2:], slot="wqb")
    e_wkvb = p.gdma(lambda e: e.dma_start(
        c.sb_wkvb.rearrange("p (cc m) -> p cc m", m=4096),
        c.wkv_b.rearrange("(cc p) m -> p cc m", p=128)),
        after=c.e_kpe[-2:], slot="wkvb")
    t_r = v(c.pool, 49152, 128, 512, F32)
    t_i = v(c.pool, 51200, 128, 512, F32)
    m_a = v(c.pool, 53248, 128, 512, F32)
    m_b = v(c.pool, 55296, 128, 512, F32)
    outr = v(c.pool, 57344, 128, 512, BF16)
    outi = v(c.pool, 58368, 128, 512, BF16)
    stg = [v(c.pool, 59392 + i * 1024, 128, 512, BF16) for i in range(2)]

    c.kv_nopeT = v(c.biga, 0, 128, 32768, BF16)
    last_bank_user = {}
    hold = {}
    sbi = [0]

    def mms(bk, lhs_fn, rhs_fn, extra_first):
        last = None
        for qc in range(4):
            aft = [] if qc else (extra_first + [last_bank_user.get(bk)])
            last = p.t(lambda e, l=lhs_fn(qc), r=rhs_fn(qc), q_=qc: e.matmul(
                c.bank(bk), l, r, start=(q_ == 0), stop=(q_ == 3),
                skip_group_check=True), after=aft)
        return last

    c.e_qspill = []
    step = [0]
    oc_order = list(range(16)) + [x for rc in range(4) for x in (16 + rc, 20 + rc)]
    for ts in range(NQ):
        for oc in oc_order:
            bk = step[0] % 2
            step[0] += 1
            last_mm = mms(bk,
                          lambda qc, o=oc: c.sb_wqb[:, qc * 3072 + o * 128:
                                                    qc * 3072 + o * 128 + 128],
                          lambda qc, t_=ts: c.q_aT[:, qc * 2048 + t_ * 512:
                                                   qc * 2048 + t_ * 512 + 512],
                          [e_wqb])
            if oc < 16:
                st = stg[sbi[0]]
                key = ("stg", sbi[0])
                sbi[0] ^= 1
                e_ev = p.s(lambda e, st_=st, o=oc, k_=bk: e.activation(
                    st_, c.bank(k_), AF.Identity, bias=c.sb_b_qb[:, o:o + 1],
                    scale=1.0), after=[last_mm, hold.get(key)])
                last_bank_user[bk] = e_ev
                e_sp = p.sydma(lambda e, o=oc, t_=ts, st_=st: e.dma_start(
                    c.q_scr[t_, o, 0:128, :], st_), after=[e_ev],
                    slot=f"stg{sbi[0] ^ 1}")
                hold[key] = e_sp
                c.e_qspill.append(e_sp)
            elif oc < 20:
                e_ev = p.s(lambda e, o=oc, k_=bk: e.activation(
                    t_r, c.bank(k_), AF.Identity, bias=c.sb_b_qb[:, o:o + 1],
                    scale=1.0), after=[last_mm, hold.get("rope_ma")])
                last_bank_user[bk] = e_ev
                hold["r"] = e_ev
            else:
                rc = oc - 20
                e_ev = p.s(lambda e, o=oc, k_=bk: e.activation(
                    t_i, c.bank(k_), AF.Identity, bias=c.sb_b_qb[:, o:o + 1],
                    scale=1.0), after=[last_mm, hold.get("rope_mb")])
                last_bank_user[bk] = e_ev
                sl = slice(ts * 512, ts * 512 + 512)
                ma = p.v(lambda e, b_=c.sb_cb[:, sl]: e.tensor_tensor(
                    m_a, t_r, b_, ALU.mult),
                    after=[hold["r"], c.e_cs, hold.get("spill")])
                mb = p.v(lambda e, b_=c.sb_sb2[:, sl]: e.tensor_tensor(
                    m_b, t_i, b_, ALU.mult),
                    after=[e_ev, hold.get("ei")])
                er_ = p.v(lambda e: e.tensor_tensor(outr, m_a, m_b,
                                                    ALU.subtract),
                          after=[ma, mb, hold.get("spill_r")])
                ma2 = p.v(lambda e, b_=c.sb_sb2[:, sl]: e.tensor_tensor(
                    m_a, t_r, b_, ALU.mult), after=[er_])
                mb2 = p.v(lambda e, b_=c.sb_cb[:, sl]: e.tensor_tensor(
                    m_b, t_i, b_, ALU.mult), after=[er_])
                ei_ = p.v(lambda e: e.tensor_tensor(outi, m_a, m_b, ALU.add),
                          after=[ma2, mb2, hold.get("spill_i")])
                hold["ei"] = ei_
                hold["rope_ma"] = ma2
                hold["rope_mb"] = mb2
                sps = []
                for hh in range(4):
                    h = rc * 4 + hh
                    sps.append(p.sydma(
                        lambda e, h_=h, t_=ts, hh_=hh: e.dma_start(
                            c.q_scr[t_, h_, 128:160, :],
                            outr[hh_ * 32:hh_ * 32 + 32, :]),
                        after=[er_] if hh == 0 else [], slot="qper"))
                for hh in range(4):
                    h = rc * 4 + hh
                    sps.append(p.sydma(
                        lambda e, h_=h, t_=ts, hh_=hh: e.dma_start(
                            c.q_scr[t_, h_, 160:192, :],
                            outi[hh_ * 32:hh_ * 32 + 32, :]),
                        after=[ei_] if hh == 0 else [], slot="qpei"))
                hold["spill_r"] = sps[3]
                hold["spill_i"] = sps[7]
                hold["spill"] = sps[7]
                c.e_qspill += sps

    c.e_kvn = []
    for ts in range(NQ):
        for oc in range(16):
            bk = 2 + step[0] % 2
            step[0] += 1
            last_mm = mms(bk,
                          lambda qc, o=oc: c.sb_wkvb[:, qc * 4096 + o * 128:
                                                     qc * 4096 + o * 128 + 128],
                          lambda qc, t_=ts: c.ckvT[:, qc * 2048 + t_ * 512:
                                                   qc * 2048 + t_ * 512 + 512],
                          [e_wkvb])
            e_ev = p.s(lambda e, o=oc, t_=ts, k_=bk: e.activation(
                c.kv_nopeT[:, o * 2048 + t_ * 512: o * 2048 + t_ * 512 + 512],
                c.bank(k_), AF.Identity, bias=c.sb_b_kvb[:, o:o + 1],
                scale=1.0), after=[last_mm])
            last_bank_user[bk] = e_ev
            c.e_kvn.append(e_ev)
    c.e_vspill = []
    for kt in range(NTT):
        for vs in range(4):
            bk = 4 + step[0] % 2
            step[0] += 1
            last_mm = mms(bk,
                          lambda qc, k_=kt: c.ckvT[:, qc * 2048 + k_ * 128:
                                                   qc * 2048 + k_ * 128 + 128],
                          lambda qc, v_=vs: c.sb_wkvb[
                              :, qc * 4096 + 2048 + v_ * 512:
                              qc * 4096 + 2048 + v_ * 512 + 512],
                          [e_wkvb])
            st = stg[sbi[0]]
            key = ("stg", sbi[0])
            sbi[0] ^= 1
            e_ev = p.s(lambda e, st_=st, k_=bk: e.activation(
                st_, c.bank(k_), AF.Identity, scale=1.0),
                after=[last_mm, hold.get(key)])
            last_bank_user[bk] = e_ev
            sps = []
            for hh in range(4):
                h = vs * 4 + hh
                sps.append(p.sydma(
                    lambda e, h_=h, k_=kt, hh_=hh, st_=st: e.dma_start(
                        c.v_scr[h_, k_ * 128:(k_ + 1) * 128, :],
                        st_[:, hh_ * 128:hh_ * 128 + 128]),
                    after=[e_ev] if hh == 0 else [],
                    slot=f"stg{sbi[0] ^ 1}"))
            hold[key] = sps[-1]
            c.e_vspill += sps
    c.e_p2b_banks = [last_bank_user.get(i) for i in range(6)]
    if "p2b" in c.dbg:
        _dbg_out(c, "dbg_kvn", c.kv_nopeT, [128, 32768], BF16, c.e_kvn[-1:])
        _dbg_out(c, "dbg_qpage", c.q_scr[1, 5], [192, 512], BF16, c.e_qspill)
        _dbg_out(c, "dbg_vpage", c.v_scr[3], [2048, 128], BF16, c.e_vspill)


def _tern_emitters(c):
    p, v = c.p, c.view
    wb = [v(c.pool, 35840 + i * 4096, 128, 2048, BF16) for i in range(2)]
    w1 = v(c.pool, 44032, 128, 2048, BF16)
    m1 = v(c.pool, 48128, 128, 2048, BF16)
    m2 = v(c.pool, 52224, 128, 2048, BF16)
    gs = v(c.pool, 56320, 128, 32, F32)
    scb = v(c.pool, 56448, 128, 32, BF16)
    thb = v(c.pool, 56512, 128, 32, BF16)
    nthb = v(c.pool, 56576, 128, 32, BF16)
    tstg = [v(c.pool, 56704 + i * 2048, 128, 1024, BF16) for i in range(2)]
    guard = list(c.e_p2b_banks) + list(c.e_qspill) + list(c.e_vspill)
    st = {"load": {0: guard, 1: guard}, "w1_user": guard,
          ("stg", 0): guard, ("stg", 1): guard, "th_user": guard, "tsi": 0}
    c.b7 = {"last": guard}
    c.tern_spills = {}
    c.tern_done = {}

    def step(i):
        fc = i < 8
        if fc:
            src = c.wsl1[i * 128:(i + 1) * 128, :]
        else:
            src = c.wsl2[(i - 8) * 128:(i - 8 + 1) * 128, :]
        bi = i % 2
        e_ld = p.gdma(lambda e, s_=src, b=bi: e.dma_start(wb[b], s_),
                      after=[st["load"].get(bi)], slot=f"wb{bi}")
        wb3 = wb[bi].rearrange("p (gk k) -> p gk k", k=64)
        e_rd = p.v(lambda e, b=bi: e.tensor_reduce(
            gs, wb[b].rearrange("p (gk k) -> p gk k", k=64), AX.X, ALU.add,
            apply_absolute_value=True), after=[e_ld, st["th_user"]])
        e_sc = p.v(lambda e: e.tensor_scalar(scb, gs, 1.0 / 64.0, 1e-8,
                                             ALU.mult, ALU.max), after=[e_rd])
        e_th = p.v(lambda e: e.tensor_scalar(thb, scb, 0.5, None, ALU.mult),
                   after=[e_sc])
        e_nth = p.v(lambda e: e.tensor_scalar(nthb, scb, -0.5, None, ALU.mult),
                    after=[e_sc])
        th_b = thb.unsqueeze(2).broadcast_to([128, 32, 64])
        nth_b = nthb.unsqueeze(2).broadcast_to([128, 32, 64])
        sc_b = scb.unsqueeze(2).broadcast_to([128, 32, 64])
        m13 = m1.rearrange("p (gk k) -> p gk k", k=64)
        m23 = m2.rearrange("p (gk k) -> p gk k", k=64)
        w13 = w1.rearrange("p (gk k) -> p gk k", k=64)
        e_m1 = p.v(lambda e, b=bi: e.tensor_tensor(
            m13, wb[b].rearrange("p (gk k) -> p gk k", k=64), th_b, ALU.is_gt),
            after=[e_th, st["w1_user"]])
        e_m2 = p.v(lambda e, b=bi: e.tensor_tensor(
            m23, wb[b].rearrange("p (gk k) -> p gk k", k=64), nth_b, ALU.is_lt),
            after=[e_nth])
        e_q = p.v(lambda e: e.scalar_tensor_tensor(w1, m1, 1.0, m2, ALU.mult,
                                                   ALU.subtract),
                  after=[e_m1, e_m2, st["w1_user"]])
        e_w1 = p.v(lambda e, b=bi: e.tensor_tensor(
            wb[b].rearrange("p (gk k) -> p gk k", k=64),
            w1.rearrange("p (gk k) -> p gk k", k=64), sc_b, ALU.mult),
            after=[e_q])
        st["w1_user"] = e_w1
        st["th_user"] = e_m2
        return i, e_w1

    def stepB(item):
        i, e_w1 = item
        bi = i % 2
        fc = i < 8
        sp_last = None
        tevs = None
        for half in range(2):
            ti = st["tsi"]
            st["tsi"] ^= 1
            for j in range(8):
                blk = wb[bi][:, (half * 8 + j) * 128:(half * 8 + j) * 128 + 128]
                aft = [e_w1, c.b7["last"]] if j == 0 else []
                tevs = p.t(lambda e, b_=blk, jj=j: e.transpose(
                    c.bank_bf(7)[:, jj * 128:jj * 128 + 128], b_, c.sb_idb),
                    after=aft)
            e_cp = p.v(lambda e, t_=ti: e.tensor_copy(tstg[t_], c.bank_bf(7)),
                       after=[tevs, st[("stg", ti)]])
            c.b7["last"] = e_cp
            if fc:
                sp_last = p.gdma(lambda e, h_=half, i_=i, t_=ti: e.dma_start(
                    c.gin1[i_].rearrange("p (n j) -> p n j", j=128)[
                        :, h_ * 8:(h_ + 1) * 8, :],
                    tstg[t_].rearrange("p (n j) -> p n j", j=128)),
                    after=[e_cp], slot=f"tstg{ti}")
            else:
                dcl = i - 8
                for j in range(8):
                    jj = half * 8 + j
                    sp_last = p.gdma(lambda e, jj_=jj, d_=dcl, t_=ti, j_=j:
                                     e.dma_start(
                        c.gin2[jj_, :, d_ * 128:(d_ + 1) * 128],
                        tstg[t_][:, j_ * 128:j_ * 128 + 128]),
                        after=[e_cp] if j == 0 else [], slot=f"tstg{ti}")
            st[("stg", ti)] = sp_last
            c.tern_spills[ti] = sp_last
            c.tern_done.setdefault(i, []).append(sp_last)
        st["load"][bi] = tevs
        return sp_last

    return step, stepB


def _p3(c):
    p, v = c.p, c.view
    qn = [v(c.pool, 16384 + i * 1024, 128, 512, BF16) for i in range(2)]
    qp = [v(c.pool, 18432 + i * 1024, 64, 512, BF16) for i in range(2)]
    vh = [v(c.pool, 20480 + i * 2048, 128, 1024, BF16) for i in range(2)]
    Ering = [v(c.pool, 24576 + i * 1024, 128, 512, BF16) for i in range(3)]
    srow = v(c.pool, 27648, 1, 512, F32)
    rrow = v(c.pool, 29696, 1, 512, F32)
    rb = v(c.pool, 31744, 128, 512, F32)
    otmp = v(c.pool, 33792, 128, 512, F32)
    c.O_sb = v(c.bigb, 0, 128, 32768, BF16)
    tern, ternB = _tern_emitters(c)
    c.e_tern = None
    tern_i = [0]
    pend_tern = []
    st = {}
    it = 0
    c.e_attn = None
    c.e_o_list = []
    pend_tail = []
    for k, ev in enumerate(c.e_p2b_banks):
        st[("bank", k)] = ev

    def tail_part1(item):
        h, qt, kb, ob, e_sum, e_pv = item
        e_sr = p.s(lambda e, kb_=kb: e.copy(srow, c.bank(kb_)[0:1, :]),
                   after=[e_sum, st.get("srow_user")])
        st[("bank", kb)] = e_sr
        e_ln = p.s(lambda e: e.activation(rrow, srow, AF.Ln), after=[e_sr])
        e_rx = p.s(lambda e: e.activation(srow, rrow, AF.Exp, scale=-1.0),
                   after=[e_ln])
        return e_rx

    def tail_part2(item, e_rx):
        h, qt, kb, ob, e_sum, e_pv = item
        e_bc = p.t(lambda e, kb_=kb: e.matmul(c.bank(kb_), c.sb_ones_row, srow,
                                              start=True, stop=True,
                                              skip_group_check=True),
                   after=[e_rx, c.e_onesr])
        st["srow_user"] = e_bc
        e_rbe = p.v(lambda e, kb_=kb: e.tensor_copy(rb, c.bank(kb_)),
                    after=[e_bc, st.get("rb_user")])
        st[("bank", kb)] = e_rbe
        e_mul = p.v(lambda e, ob_=ob: e.tensor_tensor(otmp, c.bank(ob_), rb,
                                                      ALU.mult),
                    after=[e_pv, e_rbe, st.get("otmp_user")])
        st[("bank", ob)] = e_mul
        st["rb_user"] = e_mul
        e_o = p.v(lambda e, h_=h, q_=qt: e.tensor_scalar(
            c.O_sb[:, h_ * 2048 + q_ * 512: h_ * 2048 + q_ * 512 + 512],
            otmp, c.sb_b_kvb[:, 16 + h_:17 + h_], None, ALU.add),
            after=[e_mul])
        st["otmp_user"] = e_o
        c.e_attn = e_o
        c.e_o_list.append(e_o)
    SB3 = (0, 1, 6)
    pend_rx = [None]
    for qt in range(NQ):
        for h in range(NH):
            bi = it % 2
            par = it % 2
            ob = 2 if par == 0 else 4
            kb = 3 if par == 0 else 5
            if pend_tail:
                pend_rx[0] = tail_part1(pend_tail[0])
            e_qn = p.sydma(lambda e, q_=qt, h_=h, b=bi: e.dma_start(
                qn[b], c.q_scr[q_, h_, 0:128, :]),
                after=[st.get(("qn", bi))] +
                      (c.e_qspill if it == 0 else []),
                slot=f"qn{bi}")
            e_qp = p.sydma(lambda e, q_=qt, h_=h, b=bi: e.dma_start(
                qp[b], c.q_scr[q_, h_, 128:192, :]),
                after=[st.get(("qn", bi))], slot=f"qp{bi}")
            e_v0 = p.sydma(lambda e, h_=h: e.dma_start(
                vh[0].rearrange("p (n vv) -> p n vv", vv=128),
                c.v_scr[h_, 0:1024, :].rearrange("(n p) vv -> p n vv", p=128)),
                after=[st.get("v0")] + (c.e_vspill if it == 0 else []),
                slot="vh0")
            e_v1 = p.sydma(lambda e, h_=h: e.dma_start(
                vh[1].rearrange("p (n vv) -> p n vv", vv=128),
                c.v_scr[h_, 1024:2048, :].rearrange("(n p) vv -> p n vv",
                                                    p=128)),
                after=[st.get("v1")] + (c.e_vspill if it == 0 else []),
                slot="vh1")
            e_exp = {}
            e_sum = e_pv = None

            def emit_spv(kt):
                nonlocal e_sum, e_pv
                aftsum = [e_exp[kt], c.e_ones]
                if kt == 0:
                    aftsum += [st.get(("bank", kb))]
                e_sum = p.t(lambda e, k_=kt, kb_=kb: e.matmul(
                    c.bank(kb_)[0:1, :], c.sb_ones_col, Ering[k_ % 3],
                    start=(k_ == 0), stop=(k_ == NTT - 1),
                    skip_group_check=True), after=aftsum)
                aftpv = []
                if kt == 0:
                    aftpv = [e_v0, st.get(("bank", ob))]
                elif kt == 8:
                    aftpv = [e_v1]
                e_pv = p.t(lambda e, k_=kt, ob_=ob: e.matmul(
                    c.bank(ob_),
                    vh[k_ // 8][:, (k_ % 8) * 128:(k_ % 8) * 128 + 128],
                    Ering[k_ % 3], start=(k_ == 0), stop=(k_ == NTT - 1),
                    skip_group_check=True), after=aftpv)
                st[("ering", kt % 3)] = e_pv
                if kt == 7:
                    st["v0"] = e_pv
                if kt == 15:
                    st["v1"] = e_pv

            for kt in range(NTT):
                bs = SB3[kt % 3]
                aft = []
                if kt == 0:
                    aft = [e_qn, e_qp, st.get(("bank", SB3[0]))]
                    if it == 0:
                        aft += [c.e_kvn[-1]] + c.e_kpe[-2:]
                elif kt <= 2:
                    aft = [st.get(("bank", SB3[kt]))]
                else:
                    aft = [e_exp[kt - 3]]
                e_s1 = p.t(lambda e, k_=kt, b=bi, bs_=bs, h_=h: e.matmul(
                    c.bank(bs_),
                    c.kv_nopeT[:, h_ * 2048 + k_ * 128: h_ * 2048 + k_ * 128 + 128],
                    qn[b], start=True, stop=False, skip_group_check=True),
                    after=aft)
                e_s2 = p.t(lambda e, k_=kt, b=bi, bs_=bs: e.matmul(
                    c.bank(bs_), c.kpeT[:, k_ * 128:k_ * 128 + 128], qp[b],
                    start=False, stop=True, skip_group_check=True))
                e_exp[kt] = p.s(lambda e, k_=kt, bs_=bs: e.activation(
                    Ering[k_ % 3], c.bank(bs_), AF.Exp, scale=SM_SCALE),
                    after=[e_s2, st.get(("ering", kt % 3))])
                if kt >= 2:
                    emit_spv(kt - 2)
            emit_spv(NTT - 2)
            emit_spv(NTT - 1)
            st[("qn", bi)] = e_pv
            st[("bank", SB3[0])] = e_exp[13]
            st[("bank", SB3[1])] = e_exp[14]
            st[("bank", SB3[2])] = e_exp[15]
            if pend_tail:
                tail_part2(pend_tail.pop(0), pend_rx[0])
            pend_tail.append((h, qt, kb, ob, e_sum, e_pv))
            while pend_tern and pend_tern[0][1] < it:
                c.e_tern = ternB(pend_tern.pop(0)[0])
            for _ in range(1 if it % 2 == 0 else 2):
                if tern_i[0] < 16:
                    pend_tern.append((tern(tern_i[0]), it))
                    tern_i[0] += 1
            it += 1
    while pend_tail:
        e_rx = tail_part1(pend_tail[0])
        tail_part2(pend_tail.pop(0), e_rx)
    while pend_tern:
        c.e_tern = ternB(pend_tern.pop(0)[0])
    grp = [list(range(c.ncc))]
    anchors1 = (18, 24, 30, 36)
    anchors2 = (40, 45, 50, 55)
    e_cc = None
    for pidx in range(4):
        a = (c.e_o_list[anchors1[pidx]]
             if len(c.e_o_list) > anchors1[pidx] else None)
        outsl = c.gout1[pidx * 2048: pidx * 2048 + c.ncc * 256, :]
        e_cc = p.g(lambda e, p_=pidx, o_=outsl: e.collective_compute(
            "AllGather", ALU.bypass, replica_groups=grp,
            ins=[c.gin1[2 * p_:2 * p_ + 2].opt()], outs=[o_.opt()]),
            after=[a, e_cc, c.e_tern] +
                  c.tern_done.get(2 * pidx, []) +
                  c.tern_done.get(2 * pidx + 1, []))
    for pidx in range(4):
        a = (c.e_o_list[anchors2[pidx]]
             if len(c.e_o_list) > anchors2[pidx] else None)
        outsl = c.gout2[pidx * 4096: pidx * 4096 + c.ncc * 512, :]
        e_cc = p.g(lambda e, p_=pidx, o_=outsl: e.collective_compute(
            "AllGather", ALU.bypass, replica_groups=grp,
            ins=[c.gin2[4 * p_:4 * p_ + 4].opt()], outs=[o_.opt()]),
            after=[a, e_cc] + list(c.tern_spills.values()))
    c.e_cc = e_cc
    c.e_p3_banks = {k: st.get(("bank", k)) for k in range(8)}
    c.e_p3_banks[6] = st.get(("bank", 6))
    c.e_p3_banks[7] = c.b7["last"]
    if "p3" in c.dbg:
        _dbg_out(c, "dbg_O", c.O_sb, [128, 32768], BF16, [c.e_attn])
        _dbg_out(c, "dbg_w1blk", c.gout1[2 * 2048 + 128:2 * 2048 + 256, :],
                 [128, 2048], BF16, [c.e_cc])
        _dbg_out(c, "dbg_w2blk", c.gout2[4096 + 3 * 128:4096 + 4 * 128, :],
                 [128, 1024], BF16, [c.e_cc])


def _p4(c):
    p, v = c.p, c.view
    wo_col = [v(c.pool, 44032 + i * 4096, 128, 2048, BF16) for i in range(2)]
    ev_sb = v(c.pool, 24576, 128, 2048, F32)
    stg4 = [v(c.pool, 32768 + i * 2048, 128, 512, F32) for i in range(4)]
    st = {}
    e_last_spill = None
    for dc in range(DC):
        bi = dc % 2
        src = c.wo[:, dc * 128:(dc + 1) * 128].rearrange(
            "(hc p) j -> p hc j", p=128)
        e_w = p.gdma(lambda e, s_=src, b=bi: e.dma_start(
            wo_col[b].rearrange("p (hc j) -> p hc j", j=128), s_),
            after=[st.get(("w", bi))] +
                  ([c.e_tern] if dc < 2 else []), slot=f"wo{bi}")
        last = {}
        for hcc in range(16):
            for j in range(4):
                aft = []
                if hcc == 0:
                    aft = [e_w, st.get(("bank", j), c.e_p3_banks.get(j))]
                    if dc == 0:
                        aft += [c.e_attn]
                last[j] = p.t(lambda e, h_=hcc, jj=j, b=bi: e.matmul(
                    c.bank(jj), wo_col[b][:, h_ * 128:h_ * 128 + 128],
                    c.O_sb[:, h_ * 2048 + jj * 512: h_ * 2048 + jj * 512 + 512],
                    start=(h_ == 0), stop=(h_ == 15), skip_group_check=True),
                    after=aft)
        st[("w", bi)] = last[3]
        e_evs = {}
        for j in range(4):
            e_evs[j] = p.s(lambda e, jj=j: e.activation(
                ev_sb[:, jj * 512:(jj + 1) * 512], c.bank(jj), AF.Identity,
                scale=1.0), after=[last[j], st.get("evsb_user")])
            st[("bank", j)] = e_evs[j]
        tp_last = {}
        for tt in range(16):
            b4 = 4 + tt // 4
            aft = [e_evs[tt // 4]]
            if tt % 4 == 0:
                aft += [st.get(("bank", b4), c.e_p3_banks.get(b4)), c.e_idf]
            tp_last[b4] = p.t(lambda e, t_=tt, b_=b4: e.transpose(
                c.bank(b_)[:, (t_ % 4) * 128:(t_ % 4) * 128 + 128],
                ev_sb[:, t_ * 128:t_ * 128 + 128], c.sb_idf), after=aft)
        st["evsb_user"] = tp_last[7]
        for gi in range(4):
            e_cp = p.v(lambda e, g_=gi: e.tensor_copy(
                stg4[g_], c.bank(4 + g_)),
                after=[tp_last[4 + gi], st.get(("stg", gi))] +
                      ([c.e_attn, c.e_tern] if dc == 0 else []))
            st[("bank", 4 + gi)] = e_cp
            for u in range(4):
                tt = gi * 4 + u
                e_last_spill = p.sydma(
                    lambda e, t_=tt, d_=dc, g_=gi, u_=u: e.dma_start(
                        c.o1_scr[t_, :, d_ * 128:(d_ + 1) * 128],
                        stg4[g_][:, u_ * 128:u_ * 128 + 128]),
                    after=[e_cp] if u == 0 else [], slot=f"st4_{gi}")
            st[("stg", gi)] = e_last_spill
    c.e_p4 = e_last_spill
    c.e_p4_allspills = [st[("stg", gi)] for gi in range(4)]
    c.e_p4_banks = {k: st.get(("bank", k)) for k in range(8)}
    if "p4" in c.dbg:
        _dbg_out(c, "dbg_o1", c.o1_scr[3], [128, 2048], F32,
                 c.e_p4_allspills)
        _dbg_out(c, "dbg_evsb", ev_sb, [128, 2048], F32, c.e_p4_allspills)
        _dbg_out(c, "dbg_O2", c.O_sb, [128, 32768], BF16, c.e_p4_allspills)


def _p56(c):
    p, v = c.p, c.view
    o1b = v(c.pool, 16384, 128, 2048, F32)
    xb = v(c.pool, 24576, 128, 2048, F32)
    t0b = [v(c.pool, 32768, 128, 2048, F32), v(c.pool, 0, 128, 2048, F32)]
    ln2ob = [v(c.pool, 40960, 128, 2048, F32),
             v(c.pool, 8192, 128, 2048, F32)]
    xn3s = [v(c.pool, 49152, 128, 2048, BF16), v(c.pool, 61440, 128, 2048, BF16)]
    scr_s = v(c.pool, 53248, 128, 2048, BF16)
    scr_v = v(c.pool, 57344, 128, 2048, BF16)
    c.xn3T = v(c.biga, 0, 128, 32768, BF16)
    xnT3 = c.xn3T.rearrange("p (dcc t) -> p dcc t", t=2048)
    st = {}
    prev_evac = None
    pend = []

    def emit_tp(tt, e_ap3):
        nonlocal prev_evac

        def dst_fn(gi, t_=tt):
            return xnT3[:, gi * 8:gi * 8 + 8, t_ * 128:t_ * 128 + 128]
        aft_t = [e_ap3, prev_evac]
        if tt == 0:
            aft_t += [c.e_p4_banks.get(6), c.e_p4_banks.get(7)]
        evs = _transpose16(c, xn3s[tt % 2], dst_fn, BF16, aft_t, banks=(6, 7))
        prev_evac = evs[-1]

    for tt in range(NTT):
        pr = tt % 2
        t0 = t0b[pr]
        ln2o = ln2ob[pr]
        e_o1 = p.sydma(lambda e, t_=tt: e.dma_start(
            o1b, c.o1_scr[t_]),
            after=[c.e_p4 if tt == 0 else None, st.get("o1_user")] +
                  ((c.e_p4_allspills + list(c.tern_spills.values()) +
                    [c.e_cc]) if tt == 0 else []), slot="o1b")
        e_xb = p.sydma(lambda e, t_=tt: e.dma_start(
            xb, c.x[t_ * 128:(t_ + 1) * 128, :]), after=[st.get("xb_user")],
            slot="xb5")
        e_t0 = p.v(lambda e, t0_=t0: e.tensor_tensor(
            t0_, o1b, c.sb_wob_bc, ALU.add),
            after=[e_o1, c.e_wob, st.get(("t0_user_a", pr)),
                   st.get(("t0_user_b", pr))])
        st["o1_user"] = e_t0
        e_ap2 = _ln_stats_apply(c, t0, ln2o, 1.0 / D, LN_EPS, 2 + pr,
                                [e_t0], scr_s, scr_v,
                                apply_after=[st.get(("ln2o_user", pr))])
        e_x2 = p.v(lambda e, t0_=t0, l_=ln2o: e.tensor_tensor(
            t0_, l_, xb, ALU.add), after=[e_ap2, e_xb])
        st["xb_user"] = e_x2
        st[("ln2o_user", pr)] = e_x2
        e_sp = p.sydma(lambda e, t_=tt, t0_=t0: e.dma_start(
            c.x2_scr[t_], t0_), after=[e_x2], slot=f"x2sp{pr}")
        e_ap3 = _ln_stats_apply(c, t0, xn3s[pr], 1.0 / D, LN_EPS,
                                pr, [e_x2], scr_s, scr_v)
        st[("t0_user_a", pr)] = e_ap3
        st[("t0_user_b", pr)] = e_sp
        st["last_sp"] = e_sp
        pend.append((tt, e_ap3))
        if len(pend) > 1:
            emit_tp(*pend.pop(0))
    while pend:
        emit_tp(*pend.pop(0))
    c.e_p56 = prev_evac
    c.e_x2_spill = st[("t0_user_b", 1)]
    c.e_x2_spill2 = st[("t0_user_b", 0)]
    if "p56" in c.dbg:
        _dbg_out(c, "dbg_xn3T", c.xn3T, [128, 32768], BF16, [c.e_p56])
        _dbg_out(c, "dbg_x2", c.x2_scr[6], [128, 2048], F32,
                 [c.e_x2_spill, c.e_p56])


def _p8(c):
    p, v = c.p, c.view
    w1sb = [v(c.pool, 16384 + i * 4096, 128, 2048, BF16) for i in range(3)]
    w2h = [v(c.pool, 57472 + i * 2048, 128, 1024, BF16) for i in range(3)]
    jyst = v(c.pool, 0, 128, 4096, BF16)
    rl = v(c.pool, 28672, 128, 512, BF16)
    h2 = v(c.pool, 29696, 128, 512, BF16)
    jy = v(c.pool, 30720, 128, 1024, BF16)
    junk2 = v(c.pool, 32768, 128, 1024, BF16)
    x2b = v(c.pool, 34816, 128, 2048, F32)
    outb = v(c.pool, 43008, 128, 2048, F32)
    srow2 = v(c.pool, 51200, 1, 512, F32)
    rrow2 = v(c.pool, 53248, 1, 512, F32)
    rstd_t = v(c.pool, 55296, 128, 4, F32)
    cs_t = v(c.pool, 55360, 128, 1, F32)
    smacc = []
    off = 55424
    for i in range(4):
        s8 = Ctx()
        for nm in ("s0", "s1", "q0", "q1"):
            setattr(s8, nm, v(c.pool, off, 128, 1, F32))
            off += 64
        smacc.append(s8)
    smch = []
    for i in range(2):
        s8 = Ctx()
        for nm in ("sum", "sq", "mean", "tmp", "var", "log", "rstd", "bias"):
            setattr(s8, nm, v(c.pool, off, 128, 1, F32))
            off += 64
        smch.append(s8)
    assert off <= 57472

    h_sb = v(c.bigb, 0, 128, 32768, BF16)
    jy1 = {0: v(c.bigb, 63488, 128, 1024, BF16),
           1: v(c.bigb, 61440, 128, 1024, BF16)}
    jy2 = [jy, v(c.bigb, 59392, 128, 1024, BF16)]
    st = {("fc", 0): c.e_p4_banks.get(0), ("fc", 1): c.e_p4_banks.get(1),
          "ms": c.e_p4_banks.get(2), "rt": c.e_p4_banks.get(3)}
    pa_user = {j: c.e_p4_banks.get(4 + j) for j in range(4)}
    pa_user[0] = [c.e_p4_banks.get(0), c.e_p4_banks.get(1)]
    pa_user = {0: [c.e_p4_banks.get(0), c.e_p4_banks.get(1)],
               1: [c.e_p4_banks.get(2), c.e_p4_banks.get(3)],
               2: [c.e_p4_banks.get(4), c.e_p4_banks.get(5)],
               3: [c.e_p4_banks.get(6), c.e_p4_banks.get(7)]}
    h_guard = [c.e_attn]
    h_guard = [c.e_p4]
    prev_w1_user = [None, None, None]
    prev_w2_user = [None, None, None]
    ln4_pend = []
    for s_ in range(NSTRIP):
        e_h_prev = None
        e_summ = None
        e_rl = None
        h2ev = None
        for hcc in range(HC):
            bi = hcc % 3
            pb = hcc % 2
            aftw = [prev_w1_user[bi]]
            if s_ == 0 and hcc < 3:
                aftw += [c.e_cc, c.e_p4, c.e_p56]
            w1row = ((hcc % 8) // 2) * 2048 + (hcc // 8) * 256 + (hcc % 2) * 128
            e_w = p.sydma(lambda e, r_=w1row, b=bi: e.dma_start(
                w1sb[b], c.gout1[r_:r_ + 128, :]), after=aftw,
                slot=f"w1_{bi}")
            last_mm = None
            for dcc in range(DC):
                aft = []
                if dcc == 0:
                    aft = [e_w, st[("fc", pb)]]
                    if hcc <= 1:
                        aft += [c.e_p56 if s_ == 0 else None]
                        aft += pa_user[0] if s_ > 0 or True else []
                last_mm = p.t(lambda e, d_=dcc, h_=hcc, b=bi, pb_=pb, ss=s_:
                              e.matmul(
                    c.bank(pb_), w1sb[b][:, d_ * 128:d_ * 128 + 128],
                    c.xn3T[:, d_ * 2048 + ss * 512: d_ * 2048 + ss * 512 + 512],
                    start=(d_ == 0), stop=(d_ == DC - 1),
                    skip_group_check=True), after=aft)
            prev_w1_user[bi] = last_mm
            if hcc > 0:
                aftsum = [h2ev, c.e_ones]
                if hcc == 1:
                    aftsum += [st.get("ms")] + pa_user[1]
                e_summ = p.t(lambda e, h_=hcc - 1: e.matmul(
                    c.bank(2)[0:1, :], c.sb_ones_col, h2,
                    start=(h_ == 0), stop=(h_ == HC - 1),
                    skip_group_check=True), after=aftsum)
            e_rl = p.s(lambda e, pb_=pb: e.activation(rl, c.bank(pb_), AF.Relu),
                       after=[last_mm, e_h_prev])
            st[("fc", pb)] = e_rl
            afth = [e_rl] + (h_guard if hcc == 0 else [])
            if hcc == 58:
                afth.append(st.get("jy2b_user"))
            elif hcc == 60:
                afth.append(st.get("jy1b_user"))
            elif hcc == 62:
                afth.append(st.get("jy1a_user"))
            e_h = p.v(lambda e, h_=hcc: e.tensor_tensor(
                h_sb[:, h_ * 512:(h_ + 1) * 512], rl, rl, ALU.mult),
                after=afth)
            e_h2 = p.v(lambda e, h_=hcc: e.tensor_tensor(
                h2, h_sb[:, h_ * 512:(h_ + 1) * 512],
                h_sb[:, h_ * 512:(h_ + 1) * 512], ALU.mult),
                after=[e_h, e_summ])
            e_h_prev = e_h2
            h2ev = e_h2
            if hcc in (6, 12, 18, 24) and ln4_pend:
                ln4_pend.pop(0)()
        e_summ = p.t(lambda e: e.matmul(
            c.bank(2)[0:1, :], c.sb_ones_col, h2,
            start=False, stop=True, skip_group_check=True),
            after=[h2ev, c.e_ones])
        if s_ == 0 and "p8" in c.dbg:
            _dbg_out(c, "dbg_h", h_sb, [128, 32768], BF16, [e_h_prev])
        e_sr = p.v(lambda e: e.tensor_copy(srow2, c.bank(2)[0:1, :]),
                   after=[e_summ, st.get("srow2_user")])
        st["ms"] = e_sr
        e_ms = p.v(lambda e: e.tensor_scalar(srow2, srow2, 1.0 / HID, RMS_EPS,
                                             ALU.mult, ALU.add), after=[e_sr])
        e_ln = p.s(lambda e: e.activation(rrow2, srow2, AF.Ln), after=[e_ms])
        e_rs = p.s(lambda e: e.activation(srow2, rrow2, AF.Exp, scale=-0.5),
                   after=[e_ln])
        st["srow2_user"] = e_rs
        tp = None
        for tt in range(4):
            aft = [e_rs, c.e_idf]
            if tt == 0:
                aft += [st.get("rt")]
            tp = p.t(lambda e, t_=tt: e.transpose(
                c.bank(3)[:, t_:t_ + 1], srow2[0:1, t_ * 128:t_ * 128 + 128],
                c.sb_idf[0:1, 0:1]), after=aft)
        e_rt = p.v(lambda e: e.tensor_copy(rstd_t, c.bank(3)[:, 0:4]),
                   after=[tp, st.get("rt_user")])
        st["rt"] = e_rt
        if s_ == 0 and "p8" in c.dbg:
            _dbg_out(c, "dbg_rstd", rstd_t, [128, 4], F32, [e_rt])
            _dbg_out(c, "dbg_ms", srow2, [1, 512], F32, [e_rs])
        e_half_guard = [st[("fc", 0)], st[("fc", 1)], st.get("ms"),
                        st.get("rt"), e_h_prev] + pa_user[2] + pa_user[3]
        e_ev0 = {}
        e_sq0 = {}
        for dmh in range(2):
            last_mm = None
            for hcc in range(HC):
                bi = hcc % 3
                aftw = [prev_w2_user[bi]]
                if s_ == 0 and dmh == 0 and hcc < 3:
                    aftw += [c.e_cc, c.e_p4, c.e_p56]
                qd_, jj_ = hcc // 16, hcc % 16
                w2row = ((jj_ // 4) * 4096 + (2 * qd_ + dmh) * 512 +
                         (jj_ % 4) * 128)
                e_w2 = p.gdma(lambda e, b=bi, r_=w2row: e.dma_start(
                    w2h[b], c.gout2[r_:r_ + 128, :]),
                    after=aftw, slot=f"w2_{bi}")
                for tt in range(4):
                    for cc2 in range(2):
                        aft = []
                        if tt == 0 and cc2 == 0:
                            aft = [e_w2]
                        if hcc == 0 and cc2 == 0:
                            if dmh == 0:
                                if tt == 0:
                                    aft += e_half_guard
                            else:
                                aft += [e_ev0[tt]]
                        last_mm = p.t(lambda e, h_=hcc, t_=tt, c2=cc2, b=bi:
                                      e.matmul(
                            c.pa[t_][:, c2 * 512:c2 * 512 + 512],
                            h_sb[:, h_ * 512 + t_ * 128:
                                 h_ * 512 + t_ * 128 + 128],
                            w2h[b][:, c2 * 512:c2 * 512 + 512],
                            start=(h_ == 0), stop=(h_ == HC - 1),
                            skip_group_check=True), after=aft)
                prev_w2_user[bi] = last_mm
            if dmh == 0:
                for tt in range(4):
                    acc = smacc[tt]
                    e_ev = p.s(lambda e, t_=tt, a_=acc: e.activation(
                        jyst[:, t_ * 1024:t_ * 1024 + 1024], c.pa[t_][:],
                        AF.Identity, scale=rstd_t[:, t_:t_ + 1],
                        accum_out=a_.s0),
                        after=[last_mm, e_rt, st.get(("jyu", tt))] +
                              ([c.e_p4, c.e_p56, c.e_x2_spill,
                                c.e_x2_spill2]
                               if (s_ == 0 and tt == 0) else []))
                    e_sq = p.v(lambda e, t_=tt, a_=acc: e.scalar_tensor_tensor(
                        junk2, jyst[:, t_ * 1024:t_ * 1024 + 1024], 1.0,
                        jyst[:, t_ * 1024:t_ * 1024 + 1024], ALU.mult,
                        ALU.mult, accum_out=a_.q0),
                        after=[e_ev, st.get("junk_w"), e_h_prev] +
                              ([c.e_x2_spill, c.e_x2_spill2]
                               if (s_ == 0 and tt == 0) else []))
                    st["junk_w"] = e_sq
                    e_ev0[tt] = e_ev
                    e_sq0[tt] = e_sq
                continue
            e_ev1s = {}
            for tt in range(4):
                jbuf = jy1[tt] if tt < 2 else jy2[tt - 2]
                e_ev1s[tt] = p.s(lambda e, t_=tt, a_=smacc[tt], j_=jbuf:
                                 e.activation(
                    j_, c.pa[t_][:], AF.Identity,
                    scale=rstd_t[:, t_:t_ + 1], accum_out=a_.s1),
                    after=[last_mm, e_rt,
                           st.get(("jy2u", tt)) if tt >= 2 else None])
            pa_user[0] = [e_ev1s[0]]
            pa_user[1] = [e_ev1s[1]]

            def mk_chain(tt, s__, e_ev1, e_ev0t, e_sq0t, e_rt_):
                def emit():
                    acc = smacc[tt]
                    smm = smch[tt % 2]
                    gtt = s__ * 4 + tt
                    e_x2 = p.gdma(lambda e, g_=gtt: e.dma_start(
                        x2b, c.x2_scr[g_]),
                        after=([c.e_x2_spill, c.e_x2_spill2, c.e_p56]
                               if (s__ == 0 and tt == 0) else []) +
                              [st.get("x2_user")], slot="x2b")
                    jbuf = jy1[tt] if tt < 2 else jy2[tt - 2]
                    e_sq1 = p.v(lambda e, a_=acc, j_=jbuf:
                                e.scalar_tensor_tensor(
                        junk2, j_, 1.0, j_, ALU.mult, ALU.mult,
                        accum_out=a_.q1),
                        after=[e_ev1, st.get("junk_w")])
                    st["junk_w"] = e_sq1
                    if tt >= 2:
                        st[("jy2u", tt)] = e_sq1
                    if tt == 3:
                        st["jy2b_user"] = e_sq1
                    es1 = p.v(lambda e, m_=smm, a_=acc: e.tensor_tensor(
                        m_.sum, a_.s0, a_.s1, ALU.add),
                        after=[e_ev0t, e_ev1, st.get(("sm", tt % 2))])
                    es2 = p.v(lambda e, m_=smm, a_=acc: e.tensor_tensor(
                        m_.sq, a_.q0, a_.q1, ALU.add),
                        after=[e_sq0t, e_sq1])
                    e_rstd, e_bias = _ln_small_chain(c, smm, 1.0 / D, LN_EPS,
                                                     es1, es2)
                    e_cs2 = None
                    if tt >= 2:
                        e_cs2 = p.v(lambda e, m_=smm, t_=tt: e.tensor_tensor(
                            cs_t, rstd_t[:, t_:t_ + 1], m_.rstd, ALU.mult),
                            after=[e_rstd, st.get("cs_user"), e_rt_])
                        st["rt_user"] = e_cs2
                    e_xp = p.v(lambda e, m_=smm: e.tensor_scalar(
                        x2b, x2b, m_.bias, None, ALU.add),
                        after=[e_x2, e_bias])
                    e_oa = p.v(lambda e, t_=tt, m_=smm: e.scalar_tensor_tensor(
                        outb[:, 0:1024], jyst[:, t_ * 1024:t_ * 1024 + 1024],
                        m_.rstd, x2b[:, 0:1024], ALU.mult, ALU.add),
                        after=[e_xp, e_rstd, st.get("outb_user"), e_sq0t])
                    if tt < 2:
                        e_ob = p.v(lambda e, m_=smm, j_=jbuf:
                                   e.scalar_tensor_tensor(
                            outb[:, 1024:2048], j_, m_.rstd,
                            x2b[:, 1024:2048], ALU.mult, ALU.add),
                            after=[e_oa, e_ev1])
                        st["jy1a_user" if tt == 0 else "jy1b_user"] = e_ob
                    else:
                        e_ob = p.v(lambda e, t_=tt: e.scalar_tensor_tensor(
                            outb[:, 1024:2048], c.pa[t_][:], cs_t,
                            x2b[:, 1024:2048], ALU.mult, ALU.add),
                            after=[e_oa, e_cs2])
                        pa_user[tt] = [e_ob]
                    st[("sm", tt % 2)] = e_ob
                    st["x2_user"] = e_ob
                    if tt >= 2:
                        st["cs_user"] = e_ob
                    st[("jyu", tt)] = e_oa
                    e_od = p.gdma(lambda e, g_=gtt: e.dma_start(
                        c.out[g_ * 128:(g_ + 1) * 128, :], outb),
                        after=[e_ob], slot="outd")
                    st["outb_user"] = e_od
                    st["e_out_last"] = e_od
                return emit

            for tt in range(4):
                ln4_pend.append(mk_chain(tt, s_, e_ev1s[tt], e_ev0[tt],
                                         e_sq0[tt], e_rt))
        h_guard = [prev_w2_user[0], prev_w2_user[1], prev_w2_user[2]]
        st[("fc", 0)] = pa_user[0][0]
        st[("fc", 1)] = pa_user[0][0]
        st["ms"] = pa_user[1][0]
        st["rt"] = pa_user[1][0]
    while ln4_pend:
        ln4_pend.pop(0)()
    c.e_final = st["e_out_last"]


def _finish(c):
    nc, p = c.nc, c.p
    tail = []
    for eng in ENGS:
        last_by_slot = {}
        for r in p.recs[eng]:
            if r.dma:
                last_by_slot[r.slot] = r.ev
        tail.extend(last_by_slot.values())
    if tail:
        p.add("sy", lambda e: e.nop(), after=tail, dma=False)
    p.check()
    p.resolve()
    c.sems = {}
    for eng in ENGS:
        c.sems[("c", eng)] = c.stack.enter_context(nc.semaphore(f"c_{eng}"))
    for eng in ENGS:
        slots = sorted({r.slot for r in p.recs[eng] if r.dma})
        for sl_ in slots:
            c.sems[("d", eng, sl_)] = c.stack.enter_context(
                nc.semaphore(f"d_{eng}_{sl_}"))
    with nc.Block() as block:
        @block.tensor
        def _(t):
            p.emit("t", t, c.sems)

        @block.scalar
        def _(s):
            p.emit("s", s, c.sems)

        @block.vector
        def _(vv):
            p.emit("v", vv, c.sems)

        @block.gpsimd
        def _(g):
            p.emit("g", g, c.sems)

        @block.sync
        def _(sy):
            p.emit("sy", sy, c.sems)
    c.stack.close()


def _host_inputs(wq_a, b_qa, wq_b, b_qb, wkv_a, b_kva, wkv_b, b_kvb, wo, b_wo,
                 wfc, wproj):
    qb_perm, kvb_perm, kva_perm = _permutations()
    return {
        "wq_a": np.ascontiguousarray(np.asarray(wq_a)),
        "b_qa": np.ascontiguousarray(np.asarray(b_qa)),
        "wq_b": np.ascontiguousarray(np.asarray(wq_b)[:, qb_perm]),
        "b_qb": np.ascontiguousarray(np.asarray(b_qb)[qb_perm]),
        "wkv_a": np.ascontiguousarray(np.asarray(wkv_a)[:, kva_perm]),
        "b_kva": np.ascontiguousarray(np.asarray(b_kva)[kva_perm]),
        "wkv_b": np.ascontiguousarray(np.asarray(wkv_b)[:, kvb_perm]),
        "b_kvb": np.ascontiguousarray(np.asarray(b_kvb)[kvb_perm]),
        "wo": np.ascontiguousarray(np.asarray(wo)),
        "b_wo": np.ascontiguousarray(np.asarray(b_wo)),

        "id_f32": np.eye(128, dtype=np.float32),
        "id_bf": np.eye(128, dtype=np.float32),
        "ones_col": np.ones((128, 1), dtype=np.float32),
        "ones_row": np.ones((1, 128), dtype=np.float32),
        "rep32": np.equal(np.arange(128)[None, :] % 32,
                          np.arange(32)[:, None]).astype(np.float32),
    }


def _w_slices(wfc, wproj, core):
    wfc = np.asarray(wfc)
    wproj = np.asarray(wproj)
    wsl1 = np.ascontiguousarray(wfc[core * 1024:(core + 1) * 1024, :])
    qd, half = core // 2, core % 2
    parts = [wproj[(8 * half + k) * 128:(8 * half + k + 1) * 128,
                   qd * 2048:(qd + 1) * 2048] for k in range(8)]
    wsl2 = np.ascontiguousarray(np.concatenate(parts, axis=0))
    return wsl1, wsl2


def kernel(L0_attn_x, L0_attn_wq_a_w, L0_attn_wq_a_b, L0_attn_wq_b_w,
           L0_attn_wq_b_b, L0_attn_wkv_a_w, L0_attn_wkv_a_b, L0_attn_wkv_b_w,
           L0_attn_wkv_b_b, L0_attn_wo_w, L0_attn_wo_b, L0_ffn_W_fc,
           L0_ffn_W_proj):
    common = _host_inputs(
        L0_attn_wq_a_w, L0_attn_wq_a_b, L0_attn_wq_b_w, L0_attn_wq_b_b,
        L0_attn_wkv_a_w, L0_attn_wkv_a_b, L0_attn_wkv_b_w, L0_attn_wkv_b_b,
        L0_attn_wo_w, L0_attn_wo_b, L0_ffn_W_fc, L0_ffn_W_proj)
    nc, _ = build_program()
    xs = np.asarray(L0_attn_x)
    in_maps = []
    for b_ in range(B):
        w1s, w2s = _w_slices(L0_ffn_W_fc, L0_ffn_W_proj, b_)
        in_maps.append(dict(common, x=np.ascontiguousarray(xs[b_]),
                            wsl1=w1s, wsl2=w2s))
    res = run_bass_kernel_spmd(nc, in_maps, core_ids=list(range(B)))
    return np.stack([r["out"] for r in res.results], axis=0)


# version 17 (final)
# speedup vs baseline: 1.0071x; 1.0071x over previous
import contextlib
import numpy as np
import ml_dtypes
import concourse.bass as bass
import concourse.mybir as mybir
from concourse.bass_utils import run_bass_kernel_spmd

F32 = mybir.dt.float32
BF16 = mybir.dt.bfloat16
U8 = mybir.dt.uint8
AF = mybir.ActivationFunctionType
ALU = mybir.AluOpType
AX = mybir.AxisListType

B, T, D = 8, 2048, 2048
NH, QL, KVL, NOPE, ROPE, VH, QKH = 16, 512, 512, 128, 64, 128, 192
HID = 8192
NTT = T // 128
DC = D // 128
NQ = 4
HC = HID // 128
NSTRIP = 4
SM_SCALE = 1.0 / float(np.sqrt(128.0))
LN_EPS = 1e-5
RMS_EPS = float(np.finfo(np.float32).eps)
THETA = 1e6

ENGS = ("t", "s", "v", "g", "sy")
DRAIN_ENGS = ("s", "v", "g")


class Ev:
    __slots__ = ("eng", "idx", "rec")

    def __init__(self, eng, idx, rec):
        self.eng, self.idx, self.rec = eng, idx, rec


class Rec:
    __slots__ = ("fn", "after", "dma", "slot", "needed", "sem_val", "ev")

    def __init__(self, fn, after, dma, slot):
        self.fn, self.after, self.dma = fn, list(after), dma
        self.slot = slot
        self.needed = False
        self.sem_val = None
        self.ev = None


class Prog:

    def __init__(self):
        self.recs = {e: [] for e in ENGS}

    def add(self, eng, fn, after=(), dma=False, slot=None):
        assert (slot is not None) == dma, "DMA records need a slot"
        flat = []
        for a in after:
            if a is None:
                continue
            if isinstance(a, (list, tuple)):
                flat.extend(x for x in a if x is not None)
            else:
                flat.append(a)
        r = Rec(fn, flat, dma, slot)
        lst = self.recs[eng]
        ev = Ev(eng, len(lst), r)
        r.ev = ev
        lst.append(r)
        return ev

    def t(self, fn, after=()):
        return self.add("t", fn, after)

    def s(self, fn, after=()):
        return self.add("s", fn, after)

    def v(self, fn, after=()):
        return self.add("v", fn, after)

    def g(self, fn, after=()):
        return self.add("g", fn, after)

    def gdma(self, fn, after=(), slot=None):
        return self.add("g", fn, after, dma=True, slot=slot)

    def sydma(self, fn, after=(), slot=None):
        return self.add("sy", fn, after, dma=True, slot=slot)

    def vdma(self, fn, after=(), slot=None):
        return self.add("v", fn, after, dma=True, slot=slot)

    def resolve(self):
        for eng in ENGS:
            for r in self.recs[eng]:
                if r.dma:
                    r.needed = True
                for ev in r.after:
                    ev.rec.needed = True
        self.slot_final = {}
        for eng in ENGS:
            cnt = {}
            for r in self.recs[eng]:
                if r.needed:
                    key = ("d", eng, r.slot) if r.dma else ("c", eng)
                    cnt[key] = cnt.get(key, 0) + (16 if r.dma else 1)
                    r.sem_val = cnt[key]
            for key, val in cnt.items():
                if key[0] == "d":
                    self.slot_final[key] = val

    def check(self):
        ptr = {e: 0 for e in ENGS}
        done = set()
        progress = True
        while progress:
            progress = False
            for e in ENGS:
                while ptr[e] < len(self.recs[e]):
                    r = self.recs[e][ptr[e]]
                    if all((ev.eng, ev.idx) in done for ev in r.after):
                        done.add((e, ptr[e]))
                        ptr[e] += 1
                        progress = True
                    else:
                        break
        stuck = {e: ptr[e] for e in ENGS if ptr[e] < len(self.recs[e])}
        if stuck:
            msgs = []
            for e, i in stuck.items():
                r = self.recs[e][i]
                blocked = [(ev.eng, ev.idx) for ev in r.after
                           if (ev.eng, ev.idx) not in done]
                msgs.append(f"{e}:{i} waits on {blocked}")
            raise RuntimeError("deadlock in recorded program: " + "; ".join(msgs))

    def emit(self, eng, eng_obj, sems):
        done = {}
        for i, r in enumerate(self.recs[eng]):
            pend = {}
            for ev in r.after:
                if ev.rec.dma:
                    key = ("d", ev.eng, ev.rec.slot)
                    if ev.rec.slot.startswith("setup"):
                        val = self.slot_final[key]
                    else:
                        val = ev.rec.sem_val
                else:
                    key = ("c", ev.eng)
                    val = ev.rec.sem_val
                assert val is not None, f"unsignalled dep {ev.eng}:{ev.idx}"
                if pend.get(key, 0) < val:
                    pend[key] = val
            for key in sorted(pend):
                val = pend[key]
                if done.get(key, 0) < val:
                    eng_obj.wait_ge(sems[key], val)
                    done[key] = val
            inst = r.fn(eng_obj)
            if r.needed:
                key = ("d", eng, r.slot) if r.dma else ("c", eng)
                inst.then_inc(sems[key], 16 if r.dma else 1)


def _permutations():
    nope = [h * QKH + j for h in range(NH) for j in range(NOPE)]
    pe_r = [h * QKH + NOPE + 2 * f for h in range(NH) for f in range(ROPE // 2)]
    pe_i = [h * QKH + NOPE + 2 * f + 1 for h in range(NH) for f in range(ROPE // 2)]
    qb_perm = np.array(nope + pe_r + pe_i)
    knope = [h * (NOPE + VH) + j for h in range(NH) for j in range(NOPE)]
    kv = [h * (NOPE + VH) + NOPE + j for h in range(NH) for j in range(VH)]
    kvb_perm = np.array(knope + kv)
    kva_perm = np.array(
        list(range(KVL))
        + [KVL + 2 * f for f in range(ROPE // 2)]
        + [KVL + 2 * f + 1 for f in range(ROPE // 2)]
    )
    return qb_perm, kvb_perm, kva_perm


class Ctx:
    pass


def build_program(upto="all", dbg=(), ncc=8):
    nc = bass.Bass("TRN2", target_bir_lowering=False)
    c = Ctx()
    c.nc = nc
    c.p = Prog()
    c.ncc = ncc
    c.dbg = set(dbg)
    stack = contextlib.ExitStack()
    c.stack = stack

    c.x = nc.dram_tensor("x", [T, D], F32, kind="ExternalInput").ap()
    c.wq_a = nc.dram_tensor("wq_a", [D, QL], F32, kind="ExternalInput").ap()
    c.wq_b = nc.dram_tensor("wq_b", [QL, NH * QKH], F32, kind="ExternalInput").ap()
    c.wkv_a = nc.dram_tensor("wkv_a", [D, KVL + ROPE], F32, kind="ExternalInput").ap()
    c.wkv_b = nc.dram_tensor("wkv_b", [KVL, NH * (NOPE + VH)], F32,
                             kind="ExternalInput").ap()
    c.wo = nc.dram_tensor("wo", [D, D], F32, kind="ExternalInput").ap()
    c.wsl1 = nc.dram_tensor("wsl1", [1024, D], F32, kind="ExternalInput").ap()
    c.wsl2 = nc.dram_tensor("wsl2", [1024, D], F32, kind="ExternalInput").ap()
    c.b_qa = nc.dram_tensor("b_qa", [QL], F32, kind="ExternalInput").ap()
    c.b_qb = nc.dram_tensor("b_qb", [NH * QKH], F32, kind="ExternalInput").ap()
    c.b_kva = nc.dram_tensor("b_kva", [KVL + ROPE], F32, kind="ExternalInput").ap()
    c.b_kvb = nc.dram_tensor("b_kvb", [NH * (NOPE + VH)], F32,
                             kind="ExternalInput").ap()
    c.b_wo = nc.dram_tensor("b_wo", [D], F32, kind="ExternalInput").ap()
    c.id_f32 = nc.dram_tensor("id_f32", [128, 128], F32, kind="ExternalInput").ap()
    c.id_bf = nc.dram_tensor("id_bf", [128, 128], F32, kind="ExternalInput").ap()
    c.ones_col = nc.dram_tensor("ones_col", [128, 1], F32, kind="ExternalInput").ap()
    c.ones_row = nc.dram_tensor("ones_row", [1, 128], F32, kind="ExternalInput").ap()
    c.rep32 = nc.dram_tensor("rep32", [32, 128], F32, kind="ExternalInput").ap()
    c.out = nc.dram_tensor("out", [T, D], F32, kind="ExternalOutput").ap()

    c.q_scr = nc.dram_tensor("q_scr", [NQ, NH, 192, 512], BF16, kind="Internal").ap()
    c.v_scr = nc.dram_tensor("v_scr", [NH, T, VH], BF16, kind="Internal").ap()
    c.x2_scr = nc.dram_tensor("x2_scr", [NTT, 128, D], F32, kind="Internal").ap()
    c.o1_scr = nc.dram_tensor("o1_scr", [NTT, 128, D], F32, kind="Internal").ap()
    c.gin1 = nc.dram_tensor("gin1", [8, 128, D], BF16, kind="Internal").ap()
    c.gin2 = nc.dram_tensor("gin2", [16, 128, 1024], BF16, kind="Internal").ap()
    c.gout1 = nc.dram_tensor("gout1", [HC * 128, D], BF16, kind="Internal").ap()
    c.gout2 = nc.dram_tensor("gout2", [16384, 1024], BF16, kind="Internal").ap()

    c.biga = stack.enter_context(nc.sbuf_tensor("biga", [128, 65536], U8))
    c.bigb = stack.enter_context(nc.sbuf_tensor("bigb", [128, 65536], U8))
    c.pool = stack.enter_context(nc.sbuf_tensor("pool", [128, 65536], U8))
    c.cons = stack.enter_context(nc.sbuf_tensor("cons", [128, 13312], U8))

    c.pa = [stack.enter_context(nc.psum_tensor(f"pa{i}", [128, 1024], F32))
            for i in range(4)]

    def bank(k):
        return c.pa[k // 2][:, (k % 2) * 512:(k % 2) * 512 + 512]

    def bank_bf(k):
        return c.pa[k // 2][:].bitcast(BF16)[:, (k % 2) * 1024:(k % 2) * 1024 + 1024]

    c.bank, c.bank_bf = bank, bank_bf


    def view(arena, off, parts, free, dt):
        nb = free * mybir.dt.size(dt)
        assert off % 64 == 0 and off + nb <= arena.shape[1], (off, nb)
        return arena[0:parts, off:off + nb].bitcast(dt)

    c.view = view

    _build_consts(c)
    _p0(c)
    if upto == "p0":
        _finish(c)
        return nc, c
    _p1(c)
    if upto == "p1":
        _finish(c)
        return nc, c
    _p2a(c)
    if upto == "p2a":
        _finish(c)
        return nc, c
    _p2b(c)
    if upto == "p2b":
        _finish(c)
        return nc, c
    _p3(c)
    if upto == "p3":
        _finish(c)
        return nc, c
    _p4(c)
    if upto == "p4":
        _finish(c)
        return nc, c
    _p56(c)
    if upto == "p56":
        _finish(c)
        return nc, c
    _p8(c)
    _finish(c)
    return nc, c


def _build_consts(c):
    v = c.view
    off = 0

    def alloc(free, dt, parts=128):
        nonlocal off
        tt = v(c.cons, off, parts, free, dt)
        off += ((free * mybir.dt.size(dt) + 63) // 64) * 64
        return tt

    c.sb_idf = alloc(128, F32)
    c.sb_idb = alloc(128, BF16)
    c.sb_ones_col = alloc(1, BF16)
    c.sb_ones_row = alloc(128, F32, parts=1)
    c.sb_rep32 = alloc(128, F32, parts=32)
    c.sb_b_qa = alloc(4, F32)
    c.sb_b_qb = alloc(24, F32)
    c.sb_b_kva = alloc(4, F32)
    c.sb_b_kpe = alloc(1, F32, parts=64)
    c.sb_b_kvb = alloc(32, F32)
    c.sb_wob_bc = alloc(2048, F32)
    c.sm = []
    for i in range(4):
        s = Ctx()
        for nm in ("sum", "sq", "mean", "tmp", "var", "log", "rstd", "bias"):
            setattr(s, nm, alloc(1, F32))
        c.sm.append(s)
    assert off <= c.cons.shape[1], off


def _ln_small_chain(c, s, n_inv, eps, e_sum, e_sq):
    p = c.p
    e1 = p.v(lambda e: e.tensor_scalar(s.mean, s.sum, n_inv, None, ALU.mult),
             after=[e_sum])
    e2 = p.v(lambda e: e.scalar_tensor_tensor(s.tmp, s.mean, -1.0, s.mean,
                                              ALU.mult, ALU.mult), after=[e1])
    e3 = p.v(lambda e: e.scalar_tensor_tensor(s.var, s.sq, n_inv, s.tmp,
                                              ALU.mult, ALU.add), after=[e2, e_sq])
    e4 = p.v(lambda e: e.tensor_scalar(s.var, s.var, eps, None, ALU.add),
             after=[e3])
    e5 = p.s(lambda e: e.activation(s.log, s.var, AF.Ln), after=[e4])
    e6 = p.s(lambda e: e.activation(s.rstd, s.log, AF.Exp, scale=-0.5),
             after=[e5])
    e7 = p.v(lambda e: e.scalar_tensor_tensor(s.bias, s.mean, -1.0, s.rstd,
                                              ALU.mult, ALU.mult),
             after=[e2, e6])
    return e6, e7


def _ln_stats_apply(c, src_ap, dst_ap, n_inv, eps, setidx, after, scr_s, scr_v,
                    apply_after=()):
    p = c.p
    s = c.sm[setidx]
    e_sum = p.s(lambda e: e.activation(scr_s, src_ap, AF.Identity,
                                       accum_out=s.sum), after=after)
    e_sq = p.v(lambda e: e.scalar_tensor_tensor(scr_v, src_ap, 1.0, src_ap,
                                                ALU.mult, ALU.mult,
                                                accum_out=s.sq), after=after)
    e_rstd, e_bias = _ln_small_chain(c, s, n_inv, eps, e_sum, e_sq)
    e8 = p.s(lambda e: e.activation(dst_ap, src_ap, AF.Identity, bias=s.bias,
                                    scale=s.rstd),
             after=[e_bias, e_rstd] + list(apply_after))
    return e8


def _transpose16(c, src_tile, dst_fn, dt, after, banks, evac_eng="s"):
    p = c.p
    ident = c.sb_idb if dt == BF16 else c.sb_idf
    per_bank = 8 if dt == BF16 else 4
    ngroups = 16 // per_bank
    group_last = {}
    for j in range(16):
        b = banks[j // per_bank]
        if dt == BF16:
            out_ap = c.bank_bf(b)[:, (j % per_bank) * 128:(j % per_bank) * 128 + 128]
        else:
            out_ap = c.bank(b)[:, (j % per_bank) * 128:(j % per_bank) * 128 + 128]
        src = src_tile[:, j * 128:j * 128 + 128]
        ev = p.t(lambda e, o=out_ap, si=src: e.transpose(o, si, ident),
                 after=(after if j == 0 else []))
        group_last[j // per_bank] = ev
    evs = []
    for gi in range(ngroups):
        b = banks[gi]
        src_ps = (c.bank_bf(b) if dt == BF16 else c.bank(b)).rearrange(
            "p (n j) -> p n j", j=128)
        if evac_eng == "s":
            ev = p.s(lambda e, d=dst_fn(gi), sp=src_ps: e.copy(d, sp),
                     after=[group_last[gi]])
        else:
            ev = p.v(lambda e, d=dst_fn(gi), sp=src_ps: e.tensor_copy(d, sp),
                     after=[group_last[gi]])
        evs.append(ev)
    return evs


def _dbg_out(c, name, src_ap, shape, dt, after):
    tt = c.nc.dram_tensor(name, shape, dt, kind="ExternalOutput").ap()
    c.p.sydma(lambda e: e.dma_start(tt, src_ap), after=after, slot="dbg")


def _p0(c):
    p, v = c.p, c.view
    sy, g = p.sydma, p.gdma
    c.e_idf = sy(lambda e: e.dma_start(c.sb_idf, c.id_f32), slot="setup")
    c.e_idb = g(lambda e: e.dma_start(c.sb_idb, c.id_bf), slot="setupg")
    c.e_ones = g(lambda e: e.dma_start(c.sb_ones_col, c.ones_col), slot="setupg")
    c.e_onesr = sy(lambda e: e.dma_start(c.sb_ones_row, c.ones_row), slot="setup")
    c.e_rep = sy(lambda e: e.dma_start(c.sb_rep32, c.rep32), slot="setup")
    for (dst, src, n) in [(c.sb_b_qa, c.b_qa, 4), (c.sb_b_qb, c.b_qb, 24),
                          (c.sb_b_kva, c.b_kva, 4), (c.sb_b_kvb, c.b_kvb, 32)]:
        for k in range(n):
            sy(lambda e, d=dst[:, k:k + 1],
               s=src[k * 128:(k + 1) * 128].unsqueeze(1): e.dma_start(d, s),
               slot="setup")
    c.e_bias = sy(lambda e: e.dma_start(c.sb_b_kpe,
                                        c.b_kva[KVL:KVL + 64].unsqueeze(1)),
                  slot="setup")
    c.sb_cb = v(c.pool, 0, 128, 2048, F32)
    c.sb_sb2 = v(c.pool, 8192, 128, 2048, F32)
    tv = v(c.pool, 16384, 1, 2048, F32)
    iv = v(c.pool, 24576, 1, 32, F32)
    ifr = v(c.pool, 24832, 1, 32, F32)
    red = v(c.pool, 25088, 32, 2048, F32)
    red2 = v(c.pool, 33280, 32, 2048, F32)
    sin32 = v(c.pool, 41472, 32, 2048, F32)
    cos32 = v(c.pool, 49664, 32, 2048, F32)
    fr = v(c.bigb, 34816, 32, 2048, F32)
    ka = v(c.bigb, 43008, 32, 2048, F32)
    kf2 = v(c.bigb, 51200, 32, 2048, F32)

    e_iota = p.g(lambda e: e.iota(tv, [[1, 2048]], channel_multiplier=0,
                                  allow_small_or_imprecise_dtypes=True))
    e_iota2 = p.g(lambda e: e.iota(iv, [[1, 32]], channel_multiplier=0,
                                   allow_small_or_imprecise_dtypes=True),
                  after=[e_iota])
    e_ifr = p.s(lambda e: e.activation(ifr, iv, AF.Exp,
                                       scale=-float(np.log(THETA) / 32)),
                after=[e_iota2])
    e_frs = []
    for j in range(4):
        e_mm = p.t(lambda e, jj=j: e.matmul(
            c.bank(0)[0:32, :], ifr, tv[:, jj * 512:(jj + 1) * 512],
            start=True, stop=True, skip_group_check=True),
            after=[e_ifr] + (e_frs[-1:] if e_frs else []))
        e_cp = p.v(lambda e, jj=j: e.tensor_copy(
            fr[:, jj * 512:(jj + 1) * 512], c.bank(0)[0:32, :]), after=[e_mm])
        e_frs.append(e_cp)
    M = 8388608.0
    TWO_PI = float(2.0 * np.pi)
    e1 = p.v(lambda e: e.tensor_scalar(ka, fr, 1.0 / TWO_PI, M, ALU.mult,
                                       ALU.add), after=[e_frs[-1]])
    e2 = p.v(lambda e: e.tensor_scalar(kf2, ka, -M, None, ALU.add), after=[e1])
    e3 = p.v(lambda e: e.scalar_tensor_tensor(red, kf2, -TWO_PI, fr, ALU.mult,
                                              ALU.add), after=[e2])
    PI = float(np.pi)
    e3b = p.v(lambda e: e.tensor_scalar(red, red, PI, -PI, ALU.min, ALU.max),
              after=[e3])
    e_sin = p.s(lambda e: e.activation(sin32, red, AF.Sin), after=[e3b])
    e4 = p.v(lambda e: e.tensor_scalar(ka, fr, 1.0 / TWO_PI, 0.25, ALU.mult,
                                       ALU.add), after=[e3])
    e5 = p.v(lambda e: e.tensor_scalar(kf2, ka, M, -M, ALU.add, ALU.add),
             after=[e4])
    e6 = p.v(lambda e: e.scalar_tensor_tensor(red2, kf2, -1.0, ka, ALU.mult,
                                              ALU.add), after=[e5])
    e7 = p.v(lambda e: e.tensor_scalar(red2, red2, TWO_PI, None, ALU.mult),
             after=[e6])
    e7b = p.v(lambda e: e.tensor_scalar(red2, red2, PI, -PI, ALU.min, ALU.max),
              after=[e7])
    e_cos = p.s(lambda e: e.activation(cos32, red2, AF.Sin), after=[e7b, e_sin])
    cb_evs = []
    for (src, dst) in [(cos32, c.sb_cb), (sin32, c.sb_sb2)]:
        for j in range(4):
            e_mm = p.t(lambda e, ss=src, jj=j: e.matmul(
                c.bank(1), c.sb_rep32, ss[:, jj * 512:(jj + 1) * 512],
                start=True, stop=True, skip_group_check=True),
                after=[e_cos, c.e_rep] + (cb_evs[-1:] if cb_evs else []))
            e_cp = p.v(lambda e, dd=dst, jj=j: e.tensor_copy(
                dd[:, jj * 512:(jj + 1) * 512], c.bank(1)), after=[e_mm])
            cb_evs.append(e_cp)
    c.e_cs = cb_evs[-1]
    wob_row = v(c.pool, 25088, 1, 2048, F32)
    e_wr = sy(lambda e: e.dma_start(wob_row, c.b_wo.unsqueeze(0)), after=[e_sin],
              slot="wob")
    wob_evs = []
    for j in range(4):
        e_mm = p.t(lambda e, jj=j: e.matmul(
            c.bank(2), c.sb_ones_row, wob_row[:, jj * 512:(jj + 1) * 512],
            start=True, stop=True, skip_group_check=True),
            after=[e_wr, c.e_onesr] + (wob_evs[-1:] if wob_evs else []))
        e_cp = p.v(lambda e, jj=j: e.tensor_copy(
            c.sb_wob_bc[:, jj * 512:(jj + 1) * 512], c.bank(2)), after=[e_mm])
        wob_evs.append(e_cp)
    c.e_wob = wob_evs[-1]
    c.sb_wqa = v(c.bigb, 0, 128, 16 * 512, BF16)
    c.sb_wkva = v(c.bigb, 16384, 128, 16 * 576, BF16)
    c.e_wqa = g(lambda e: e.dma_start(
        c.sb_wqa.rearrange("p (cc m) -> p cc m", m=512),
        c.wq_a.rearrange("(cc p) m -> p cc m", p=128)), slot="setupg")
    c.e_wkva = g(lambda e: e.dma_start(
        c.sb_wkva.rearrange("p (cc m) -> p cc m", m=576),
        c.wkv_a.rearrange("(cc p) m -> p cc m", p=128)), slot="setupg")
    if "rope" in c.dbg:
        _dbg_out(c, "dbg_cb", c.sb_cb, [128, 2048], F32, [c.e_cs])
        _dbg_out(c, "dbg_sb", c.sb_sb2, [128, 2048], F32, [c.e_cs])
    c.e_p0_free = [c.e_cs, c.e_wob]


def _p1(c):
    p, v = c.p, c.view
    c.xnT = v(c.biga, 0, 128, 32768, BF16)
    c.xbuf = [v(c.pool, 16384 + i * 8192, 128, 2048, F32) for i in range(2)]
    c.xn_tok = [v(c.pool, 32768 + i * 4096, 128, 2048, BF16) for i in range(2)]
    c.scr_s = v(c.pool, 40960, 128, 2048, BF16)
    c.scr_v = v(c.pool, 45056, 128, 2048, BF16)
    prev_apply = [None, None]
    prev_evac = None
    xnT3 = c.xnT.rearrange("p (dc t) -> p dc t", t=2048)
    pend = []

    def emit_tp(tt, e_ap):
        nonlocal prev_evac

        def dst_fn(gi, t_=tt):
            return xnT3[:, gi * 8:gi * 8 + 8, t_ * 128:t_ * 128 + 128]
        evs = _transpose16(c, c.xn_tok[tt % 2], dst_fn, BF16,
                           [e_ap, c.e_idb, prev_evac], banks=(6, 7))
        prev_evac = evs[-1]
        return evs[-1]

    for tt in range(NTT):
        bi = tt % 2
        aft = [prev_apply[bi]]
        if tt == 0:
            aft += c.e_p0_free
        e_x = p.sydma(lambda e, t_=tt, b=bi: e.dma_start(
            c.xbuf[b], c.x[t_ * 128:(t_ + 1) * 128, :]), after=aft,
            slot=f"xb{bi}")
        e_ap = _ln_stats_apply(c, c.xbuf[bi], c.xn_tok[bi], 1.0 / D, LN_EPS,
                               bi, [e_x], c.scr_s, c.scr_v)
        prev_apply[bi] = e_ap
        pend.append((tt, e_ap))
        if len(pend) > 1:
            emit_tp(*pend.pop(0))
    while pend:
        emit_tp(*pend.pop(0))
    c.e_p1 = prev_evac
    if "p1" in c.dbg:
        _dbg_out(c, "dbg_xnT", c.xnT, [128, 32768], BF16, [c.e_p1])


def _p2a(c):
    p, v = c.p, c.view
    c.q_aT = v(c.pool, 16384, 128, 4 * 2048, BF16)
    c.ckvT = v(c.pool, 32768, 128, 4 * 2048, BF16)
    c.kpe_pre = v(c.pool, 49152, 64, 2048, F32)
    c.kpeT = v(c.pool, 61440, 64, 2048, BF16)
    km = [v(c.bigb, 34816 + i * 2048, 32, 512, F32) for i in range(4)]
    last_bank_user = {}

    def proj(dst, wsb, wstride, oc, ts, bk, bias_ap, after_w, dst_col):
        last_mm = None
        for dc in range(DC):
            lhs = wsb[:, dc * wstride + oc * 128: dc * wstride + oc * 128 + 128]
            rhs = c.xnT[:, dc * 2048 + ts * 512: dc * 2048 + ts * 512 + 512]
            aft = [] if dc else [after_w, c.e_p1, last_bank_user.get(bk)]
            last_mm = p.t(lambda e, l=lhs, r=rhs, d=dc: e.matmul(
                c.bank(bk), l, r, start=(d == 0), stop=(d == DC - 1),
                skip_group_check=True), after=aft)
        e_ev = p.s(lambda e, d_=dst[:, dst_col:dst_col + 512], b_=bias_ap,
                   k_=bk: e.activation(d_, c.bank(k_), AF.Identity, bias=b_,
                                       scale=1.0), after=[last_mm, c.e_bias])
        last_bank_user[bk] = e_ev
        return e_ev

    c.e_qa, c.e_ckv = [], []
    for ts in range(NQ):
        for oc in range(4):
            c.e_qa.append(proj(c.q_aT, c.sb_wqa, 512, oc, ts,
                               (ts % 2) * 4 + oc, c.sb_b_qa[:, oc:oc + 1],
                               c.e_wqa, oc * 2048 + ts * 512))
    for ts in range(NQ):
        for oc in range(4):
            c.e_ckv.append(proj(c.ckvT, c.sb_wkva, 576, oc, ts,
                                (ts % 2) * 4 + oc, c.sb_b_kva[:, oc:oc + 1],
                                c.e_wkva, oc * 2048 + ts * 512))
    c.e_kpe = []
    for ts in range(NQ):
        last_mm = None
        for dc in range(DC):
            lhs = c.sb_wkva[:, dc * 576 + 512: dc * 576 + 576]
            rhs = c.xnT[:, dc * 2048 + ts * 512: dc * 2048 + ts * 512 + 512]
            aft = [] if dc else [c.e_wkva, c.e_p1, last_bank_user.get(0)]
            last_mm = p.t(lambda e, l=lhs, r=rhs, d=dc: e.matmul(
                c.bank(0)[0:64, :], l, r, start=(d == 0), stop=(d == DC - 1),
                skip_group_check=True), after=aft)
        e_ev = p.s(lambda e, t_=ts: e.activation(
            c.kpe_pre[:, t_ * 512:(t_ + 1) * 512], c.bank(0)[0:64, :],
            AF.Identity, bias=c.sb_b_kpe, scale=1.0),
            after=[last_mm, c.e_bias])
        last_bank_user[0] = e_ev
        sl = slice(ts * 512, ts * 512 + 512)
        kr, ki = c.kpe_pre[0:32, sl], c.kpe_pre[32:64, sl]
        cbs, sbs = c.sb_cb[0:32, sl], c.sb_sb2[0:32, sl]
        cbs32, sbs32 = c.sb_cb[32:64, sl], c.sb_sb2[32:64, sl]
        prev_o = c.e_kpe[-2:] if c.e_kpe else []
        m0 = p.v(lambda e, a_=kr, b_=cbs: e.tensor_tensor(km[0], a_, b_,
                                                          ALU.mult),
                 after=[e_ev, c.e_cs] + prev_o)
        m1 = p.v(lambda e, a_=ki, b_=sbs32: e.tensor_tensor(km[1], a_, b_,
                                                            ALU.mult),
                 after=[e_ev] + prev_o)
        m2 = p.v(lambda e, a_=kr, b_=sbs: e.tensor_tensor(km[2], a_, b_,
                                                          ALU.mult),
                 after=[e_ev] + prev_o)
        m3 = p.v(lambda e, a_=ki, b_=cbs32: e.tensor_tensor(km[3], a_, b_,
                                                            ALU.mult),
                 after=[e_ev] + prev_o)
        o0 = p.v(lambda e, d_=c.kpeT[0:32, sl]: e.tensor_tensor(
            d_, km[0], km[1], ALU.subtract), after=[m0, m1])
        o1 = p.v(lambda e, d_=c.kpeT[32:64, sl]: e.tensor_tensor(
            d_, km[2], km[3], ALU.add), after=[m2, m3])
        c.e_kpe += [o0, o1]
    if "p2a" in c.dbg:
        _dbg_out(c, "dbg_kpe_pre", c.kpe_pre, [64, 2048], F32, c.e_kpe[-2:])
        _dbg_out(c, "dbg_qaT", c.q_aT, [128, 8192], BF16, c.e_qa[-1:])
        _dbg_out(c, "dbg_ckvT", c.ckvT, [128, 8192], BF16, c.e_ckv[-1:])
        _dbg_out(c, "dbg_kpeT", c.kpeT, [64, 2048], BF16, c.e_kpe[-2:])


def _p2b(c):
    p, v = c.p, c.view
    c.sb_wqb = v(c.bigb, 0, 128, 4 * 3072, BF16)
    c.sb_wkvb = v(c.bigb, 24576, 128, 4 * 4096, BF16)
    e_wqb = p.gdma(lambda e: e.dma_start(
        c.sb_wqb.rearrange("p (cc m) -> p cc m", m=3072),
        c.wq_b.rearrange("(cc p) m -> p cc m", p=128)),
        after=[c.e_qa[-1], c.e_ckv[-1]] + c.e_kpe[-2:], slot="wqb")
    e_wkvb = p.gdma(lambda e: e.dma_start(
        c.sb_wkvb.rearrange("p (cc m) -> p cc m", m=4096),
        c.wkv_b.rearrange("(cc p) m -> p cc m", p=128)),
        after=c.e_kpe[-2:], slot="wkvb")
    t_r = v(c.pool, 49152, 128, 512, F32)
    t_i = v(c.pool, 51200, 128, 512, F32)
    m_a = v(c.pool, 53248, 128, 512, F32)
    m_b = v(c.pool, 55296, 128, 512, F32)
    outr = v(c.pool, 57344, 128, 512, BF16)
    outi = v(c.pool, 58368, 128, 512, BF16)
    stg = [v(c.pool, 59392 + i * 1024, 128, 512, BF16) for i in range(2)]

    c.kv_nopeT = v(c.biga, 0, 128, 32768, BF16)
    last_bank_user = {}
    hold = {}
    sbi = [0]

    def mms(bk, lhs_fn, rhs_fn, extra_first):
        last = None
        for qc in range(4):
            aft = [] if qc else (extra_first + [last_bank_user.get(bk)])
            last = p.t(lambda e, l=lhs_fn(qc), r=rhs_fn(qc), q_=qc: e.matmul(
                c.bank(bk), l, r, start=(q_ == 0), stop=(q_ == 3),
                skip_group_check=True), after=aft)
        return last

    c.e_qspill = []
    step = [0]
    oc_order = list(range(16)) + [x for rc in range(4) for x in (16 + rc, 20 + rc)]
    for ts in range(NQ):
        for oc in oc_order:
            bk = step[0] % 2
            step[0] += 1
            last_mm = mms(bk,
                          lambda qc, o=oc: c.sb_wqb[:, qc * 3072 + o * 128:
                                                    qc * 3072 + o * 128 + 128],
                          lambda qc, t_=ts: c.q_aT[:, qc * 2048 + t_ * 512:
                                                   qc * 2048 + t_ * 512 + 512],
                          [e_wqb])
            if oc < 16:
                st = stg[sbi[0]]
                key = ("stg", sbi[0])
                sbi[0] ^= 1
                e_ev = p.s(lambda e, st_=st, o=oc, k_=bk: e.activation(
                    st_, c.bank(k_), AF.Identity, bias=c.sb_b_qb[:, o:o + 1],
                    scale=1.0), after=[last_mm, hold.get(key)])
                last_bank_user[bk] = e_ev
                e_sp = p.sydma(lambda e, o=oc, t_=ts, st_=st: e.dma_start(
                    c.q_scr[t_, o, 0:128, :], st_), after=[e_ev],
                    slot=f"stg{sbi[0] ^ 1}")
                hold[key] = e_sp
                c.e_qspill.append(e_sp)
            elif oc < 20:
                e_ev = p.s(lambda e, o=oc, k_=bk: e.activation(
                    t_r, c.bank(k_), AF.Identity, bias=c.sb_b_qb[:, o:o + 1],
                    scale=1.0), after=[last_mm, hold.get("rope_ma")])
                last_bank_user[bk] = e_ev
                hold["r"] = e_ev
            else:
                rc = oc - 20
                e_ev = p.s(lambda e, o=oc, k_=bk: e.activation(
                    t_i, c.bank(k_), AF.Identity, bias=c.sb_b_qb[:, o:o + 1],
                    scale=1.0), after=[last_mm, hold.get("rope_mb")])
                last_bank_user[bk] = e_ev
                sl = slice(ts * 512, ts * 512 + 512)
                ma = p.v(lambda e, b_=c.sb_cb[:, sl]: e.tensor_tensor(
                    m_a, t_r, b_, ALU.mult),
                    after=[hold["r"], c.e_cs, hold.get("spill")])
                mb = p.v(lambda e, b_=c.sb_sb2[:, sl]: e.tensor_tensor(
                    m_b, t_i, b_, ALU.mult),
                    after=[e_ev, hold.get("ei")])
                er_ = p.v(lambda e: e.tensor_tensor(outr, m_a, m_b,
                                                    ALU.subtract),
                          after=[ma, mb, hold.get("spill_r")])
                ma2 = p.v(lambda e, b_=c.sb_sb2[:, sl]: e.tensor_tensor(
                    m_a, t_r, b_, ALU.mult), after=[er_])
                mb2 = p.v(lambda e, b_=c.sb_cb[:, sl]: e.tensor_tensor(
                    m_b, t_i, b_, ALU.mult), after=[er_])
                ei_ = p.v(lambda e: e.tensor_tensor(outi, m_a, m_b, ALU.add),
                          after=[ma2, mb2, hold.get("spill_i")])
                hold["ei"] = ei_
                hold["rope_ma"] = ma2
                hold["rope_mb"] = mb2
                sps = []
                for hh in range(4):
                    h = rc * 4 + hh
                    sps.append(p.sydma(
                        lambda e, h_=h, t_=ts, hh_=hh: e.dma_start(
                            c.q_scr[t_, h_, 128:160, :],
                            outr[hh_ * 32:hh_ * 32 + 32, :]),
                        after=[er_] if hh == 0 else [], slot="qper"))
                for hh in range(4):
                    h = rc * 4 + hh
                    sps.append(p.sydma(
                        lambda e, h_=h, t_=ts, hh_=hh: e.dma_start(
                            c.q_scr[t_, h_, 160:192, :],
                            outi[hh_ * 32:hh_ * 32 + 32, :]),
                        after=[ei_] if hh == 0 else [], slot="qpei"))
                hold["spill_r"] = sps[3]
                hold["spill_i"] = sps[7]
                hold["spill"] = sps[7]
                c.e_qspill += sps

    c.e_kvn = []
    for ts in range(NQ):
        for oc in range(16):
            bk = 2 + step[0] % 2
            step[0] += 1
            last_mm = mms(bk,
                          lambda qc, o=oc: c.sb_wkvb[:, qc * 4096 + o * 128:
                                                     qc * 4096 + o * 128 + 128],
                          lambda qc, t_=ts: c.ckvT[:, qc * 2048 + t_ * 512:
                                                   qc * 2048 + t_ * 512 + 512],
                          [e_wkvb])
            e_ev = p.s(lambda e, o=oc, t_=ts, k_=bk: e.activation(
                c.kv_nopeT[:, o * 2048 + t_ * 512: o * 2048 + t_ * 512 + 512],
                c.bank(k_), AF.Identity, bias=c.sb_b_kvb[:, o:o + 1],
                scale=1.0), after=[last_mm])
            last_bank_user[bk] = e_ev
            c.e_kvn.append(e_ev)
    c.e_vspill = []
    for kt in range(NTT):
        for vs in range(4):
            bk = 4 + step[0] % 2
            step[0] += 1
            last_mm = mms(bk,
                          lambda qc, k_=kt: c.ckvT[:, qc * 2048 + k_ * 128:
                                                   qc * 2048 + k_ * 128 + 128],
                          lambda qc, v_=vs: c.sb_wkvb[
                              :, qc * 4096 + 2048 + v_ * 512:
                              qc * 4096 + 2048 + v_ * 512 + 512],
                          [e_wkvb])
            st = stg[sbi[0]]
            key = ("stg", sbi[0])
            sbi[0] ^= 1
            e_ev = p.s(lambda e, st_=st, k_=bk: e.activation(
                st_, c.bank(k_), AF.Identity, scale=1.0),
                after=[last_mm, hold.get(key)])
            last_bank_user[bk] = e_ev
            sps = []
            for hh in range(4):
                h = vs * 4 + hh
                sps.append(p.sydma(
                    lambda e, h_=h, k_=kt, hh_=hh, st_=st: e.dma_start(
                        c.v_scr[h_, k_ * 128:(k_ + 1) * 128, :],
                        st_[:, hh_ * 128:hh_ * 128 + 128]),
                    after=[e_ev] if hh == 0 else [],
                    slot=f"stg{sbi[0] ^ 1}"))
            hold[key] = sps[-1]
            c.e_vspill += sps
    c.e_p2b_banks = [last_bank_user.get(i) for i in range(6)]
    if "p2b" in c.dbg:
        _dbg_out(c, "dbg_kvn", c.kv_nopeT, [128, 32768], BF16, c.e_kvn[-1:])
        _dbg_out(c, "dbg_qpage", c.q_scr[1, 5], [192, 512], BF16, c.e_qspill)
        _dbg_out(c, "dbg_vpage", c.v_scr[3], [2048, 128], BF16, c.e_vspill)


def _tern_emitters(c):
    p, v = c.p, c.view
    wb = [v(c.pool, 35840 + i * 4096, 128, 2048, BF16) for i in range(2)]
    w1 = v(c.pool, 44032, 128, 2048, BF16)
    m1 = v(c.pool, 48128, 128, 2048, BF16)
    m2 = v(c.pool, 52224, 128, 2048, BF16)
    gs = v(c.pool, 56320, 128, 32, F32)
    scb = v(c.pool, 56448, 128, 32, BF16)
    thb = v(c.pool, 56512, 128, 32, BF16)
    nthb = v(c.pool, 56576, 128, 32, BF16)
    tstg = [v(c.pool, 56704 + i * 2048, 128, 1024, BF16) for i in range(2)]
    guard = list(c.e_p2b_banks) + list(c.e_qspill) + list(c.e_vspill)
    st = {"load": {0: guard, 1: guard}, "w1_user": guard,
          ("stg", 0): guard, ("stg", 1): guard, "th_user": guard, "tsi": 0}
    c.b7 = {"last": guard}
    c.tern_spills = {}
    c.tern_done = {}

    def step(i):
        fc = i < 8
        if fc:
            src = c.wsl1[i * 128:(i + 1) * 128, :]
        else:
            src = c.wsl2[(i - 8) * 128:(i - 8 + 1) * 128, :]
        bi = i % 2
        e_ld = p.gdma(lambda e, s_=src, b=bi: e.dma_start(wb[b], s_),
                      after=[st["load"].get(bi)], slot=f"wb{bi}")
        wb3 = wb[bi].rearrange("p (gk k) -> p gk k", k=64)
        e_rd = p.v(lambda e, b=bi: e.tensor_reduce(
            gs, wb[b].rearrange("p (gk k) -> p gk k", k=64), AX.X, ALU.add,
            apply_absolute_value=True), after=[e_ld, st["th_user"]])
        e_sc = p.v(lambda e: e.tensor_scalar(scb, gs, 1.0 / 64.0, 1e-8,
                                             ALU.mult, ALU.max), after=[e_rd])
        e_th = p.v(lambda e: e.tensor_scalar(thb, scb, 0.5, None, ALU.mult),
                   after=[e_sc])
        e_nth = p.v(lambda e: e.tensor_scalar(nthb, scb, -0.5, None, ALU.mult),
                    after=[e_sc])
        th_b = thb.unsqueeze(2).broadcast_to([128, 32, 64])
        nth_b = nthb.unsqueeze(2).broadcast_to([128, 32, 64])
        sc_b = scb.unsqueeze(2).broadcast_to([128, 32, 64])
        m13 = m1.rearrange("p (gk k) -> p gk k", k=64)
        m23 = m2.rearrange("p (gk k) -> p gk k", k=64)
        w13 = w1.rearrange("p (gk k) -> p gk k", k=64)
        e_m1 = p.v(lambda e, b=bi: e.tensor_tensor(
            m13, wb[b].rearrange("p (gk k) -> p gk k", k=64), th_b, ALU.is_gt),
            after=[e_th, st["w1_user"]])
        e_m2 = p.v(lambda e, b=bi: e.tensor_tensor(
            m23, wb[b].rearrange("p (gk k) -> p gk k", k=64), nth_b, ALU.is_lt),
            after=[e_nth])
        e_q = p.v(lambda e: e.scalar_tensor_tensor(w1, m1, 1.0, m2, ALU.mult,
                                                   ALU.subtract),
                  after=[e_m1, e_m2, st["w1_user"]])
        e_w1 = p.v(lambda e, b=bi: e.tensor_tensor(
            wb[b].rearrange("p (gk k) -> p gk k", k=64),
            w1.rearrange("p (gk k) -> p gk k", k=64), sc_b, ALU.mult),
            after=[e_q])
        st["w1_user"] = e_w1
        st["th_user"] = e_m2
        return i, e_w1

    def stepB(item):
        i, e_w1 = item
        bi = i % 2
        fc = i < 8
        sp_last = None
        tevs = None
        for half in range(2):
            ti = st["tsi"]
            st["tsi"] ^= 1
            for j in range(8):
                blk = wb[bi][:, (half * 8 + j) * 128:(half * 8 + j) * 128 + 128]
                aft = [e_w1, c.b7["last"]] if j == 0 else []
                tevs = p.t(lambda e, b_=blk, jj=j: e.transpose(
                    c.bank_bf(7)[:, jj * 128:jj * 128 + 128], b_, c.sb_idb),
                    after=aft)
            e_cp = p.v(lambda e, t_=ti: e.tensor_copy(tstg[t_], c.bank_bf(7)),
                       after=[tevs, st[("stg", ti)]])
            c.b7["last"] = e_cp
            if fc:
                sp_last = p.gdma(lambda e, h_=half, i_=i, t_=ti: e.dma_start(
                    c.gin1[i_].rearrange("p (n j) -> p n j", j=128)[
                        :, h_ * 8:(h_ + 1) * 8, :],
                    tstg[t_].rearrange("p (n j) -> p n j", j=128)),
                    after=[e_cp], slot=f"tstg{ti}")
            else:
                dcl = i - 8
                for j in range(8):
                    jj = half * 8 + j
                    sp_last = p.gdma(lambda e, jj_=jj, d_=dcl, t_=ti, j_=j:
                                     e.dma_start(
                        c.gin2[jj_, :, d_ * 128:(d_ + 1) * 128],
                        tstg[t_][:, j_ * 128:j_ * 128 + 128]),
                        after=[e_cp] if j == 0 else [], slot=f"tstg{ti}")
            st[("stg", ti)] = sp_last
            c.tern_spills[ti] = sp_last
            c.tern_done.setdefault(i, []).append(sp_last)
        st["load"][bi] = tevs
        return sp_last

    return step, stepB


def _p3(c):
    p, v = c.p, c.view
    qn = [v(c.pool, 16384 + i * 1024, 128, 512, BF16) for i in range(2)]
    qp = [v(c.pool, 18432 + i * 1024, 64, 512, BF16) for i in range(2)]
    vh = [v(c.pool, 20480 + i * 2048, 128, 1024, BF16) for i in range(2)]
    Ering = [v(c.pool, 24576 + i * 1024, 128, 512, BF16) for i in range(3)]
    srow = v(c.pool, 27648, 1, 512, F32)
    rrow = v(c.pool, 29696, 1, 512, F32)
    rb = v(c.pool, 31744, 128, 512, F32)
    otmp = v(c.pool, 33792, 128, 512, F32)
    c.O_sb = v(c.bigb, 0, 128, 32768, BF16)
    tern, ternB = _tern_emitters(c)
    c.e_tern = None
    tern_i = [0]
    pend_tern = []
    st = {}
    it = 0
    c.e_attn = None
    c.e_o_list = []
    pend_tail = []
    for k, ev in enumerate(c.e_p2b_banks):
        st[("bank", k)] = ev

    def tail_part1(item):
        h, qt, kb, ob, e_sum, e_pv = item
        e_sr = p.s(lambda e, kb_=kb: e.copy(srow, c.bank(kb_)[0:1, :]),
                   after=[e_sum, st.get("srow_user")])
        st[("bank", kb)] = e_sr
        e_ln = p.s(lambda e: e.activation(rrow, srow, AF.Ln), after=[e_sr])
        e_rx = p.s(lambda e: e.activation(srow, rrow, AF.Exp, scale=-1.0),
                   after=[e_ln])
        return e_rx

    def tail_part2(item, e_rx):
        h, qt, kb, ob, e_sum, e_pv = item
        e_bc = p.t(lambda e, kb_=kb: e.matmul(c.bank(kb_), c.sb_ones_row, srow,
                                              start=True, stop=True,
                                              skip_group_check=True),
                   after=[e_rx, c.e_onesr])
        st["srow_user"] = e_bc
        e_rbe = p.v(lambda e, kb_=kb: e.tensor_copy(rb, c.bank(kb_)),
                    after=[e_bc, st.get("rb_user")])
        st[("bank", kb)] = e_rbe
        e_mul = p.v(lambda e, ob_=ob: e.tensor_tensor(otmp, c.bank(ob_), rb,
                                                      ALU.mult),
                    after=[e_pv, e_rbe, st.get("otmp_user")])
        st[("bank", ob)] = e_mul
        st["rb_user"] = e_mul
        e_o = p.v(lambda e, h_=h, q_=qt: e.tensor_scalar(
            c.O_sb[:, h_ * 2048 + q_ * 512: h_ * 2048 + q_ * 512 + 512],
            otmp, c.sb_b_kvb[:, 16 + h_:17 + h_], None, ALU.add),
            after=[e_mul])
        st["otmp_user"] = e_o
        c.e_attn = e_o
        c.e_o_list.append(e_o)
    SB3 = (0, 1, 6)
    pend_rx = [None]
    for qt in range(NQ):
        for h in range(NH):
            bi = it % 2
            par = it % 2
            ob = 2 if par == 0 else 4
            kb = 3 if par == 0 else 5
            if pend_tail:
                pend_rx[0] = tail_part1(pend_tail[0])
            e_qn = p.sydma(lambda e, q_=qt, h_=h, b=bi: e.dma_start(
                qn[b], c.q_scr[q_, h_, 0:128, :]),
                after=[st.get(("qn", bi))] +
                      (c.e_qspill if it == 0 else []),
                slot=f"qn{bi}")
            e_qp = p.sydma(lambda e, q_=qt, h_=h, b=bi: e.dma_start(
                qp[b], c.q_scr[q_, h_, 128:192, :]),
                after=[st.get(("qn", bi))], slot=f"qp{bi}")
            e_v0 = p.sydma(lambda e, h_=h: e.dma_start(
                vh[0].rearrange("p (n vv) -> p n vv", vv=128),
                c.v_scr[h_, 0:1024, :].rearrange("(n p) vv -> p n vv", p=128)),
                after=[st.get("v0")] + (c.e_vspill if it == 0 else []),
                slot="vh0")
            e_v1 = p.sydma(lambda e, h_=h: e.dma_start(
                vh[1].rearrange("p (n vv) -> p n vv", vv=128),
                c.v_scr[h_, 1024:2048, :].rearrange("(n p) vv -> p n vv",
                                                    p=128)),
                after=[st.get("v1")] + (c.e_vspill if it == 0 else []),
                slot="vh1")
            e_exp = {}
            e_sum = e_pv = None

            def emit_spv(kt):
                nonlocal e_sum, e_pv
                aftsum = [e_exp[kt], c.e_ones]
                if kt == 0:
                    aftsum += [st.get(("bank", kb))]
                e_sum = p.t(lambda e, k_=kt, kb_=kb: e.matmul(
                    c.bank(kb_)[0:1, :], c.sb_ones_col, Ering[k_ % 3],
                    start=(k_ == 0), stop=(k_ == NTT - 1),
                    skip_group_check=True), after=aftsum)
                aftpv = []
                if kt == 0:
                    aftpv = [e_v0, st.get(("bank", ob))]
                elif kt == 8:
                    aftpv = [e_v1]
                e_pv = p.t(lambda e, k_=kt, ob_=ob: e.matmul(
                    c.bank(ob_),
                    vh[k_ // 8][:, (k_ % 8) * 128:(k_ % 8) * 128 + 128],
                    Ering[k_ % 3], start=(k_ == 0), stop=(k_ == NTT - 1),
                    skip_group_check=True), after=aftpv)
                st[("ering", kt % 3)] = e_pv
                if kt == 7:
                    st["v0"] = e_pv
                if kt == 15:
                    st["v1"] = e_pv

            for kt in range(NTT):
                bs = SB3[kt % 3]
                aft = []
                if kt == 0:
                    aft = [e_qn, e_qp, st.get(("bank", SB3[0]))]
                    if it == 0:
                        aft += [c.e_kvn[-1]] + c.e_kpe[-2:]
                elif kt <= 2:
                    aft = [st.get(("bank", SB3[kt]))]
                else:
                    aft = [e_exp[kt - 3]]
                e_s1 = p.t(lambda e, k_=kt, b=bi, bs_=bs, h_=h: e.matmul(
                    c.bank(bs_),
                    c.kv_nopeT[:, h_ * 2048 + k_ * 128: h_ * 2048 + k_ * 128 + 128],
                    qn[b], start=True, stop=False, skip_group_check=True),
                    after=aft)
                e_s2 = p.t(lambda e, k_=kt, b=bi, bs_=bs: e.matmul(
                    c.bank(bs_), c.kpeT[:, k_ * 128:k_ * 128 + 128], qp[b],
                    start=False, stop=True, skip_group_check=True))
                e_exp[kt] = p.s(lambda e, k_=kt, bs_=bs: e.activation(
                    Ering[k_ % 3], c.bank(bs_), AF.Exp, scale=SM_SCALE),
                    after=[e_s2, st.get(("ering", kt % 3))])
                if kt >= 2:
                    emit_spv(kt - 2)
            emit_spv(NTT - 2)
            emit_spv(NTT - 1)
            st[("qn", bi)] = e_pv
            st[("bank", SB3[0])] = e_exp[13]
            st[("bank", SB3[1])] = e_exp[14]
            st[("bank", SB3[2])] = e_exp[15]
            if pend_tail:
                tail_part2(pend_tail.pop(0), pend_rx[0])
            pend_tail.append((h, qt, kb, ob, e_sum, e_pv))
            while pend_tern and pend_tern[0][1] < it:
                c.e_tern = ternB(pend_tern.pop(0)[0])
            for _ in range(1 if it % 2 == 0 else 2):
                if tern_i[0] < 16:
                    pend_tern.append((tern(tern_i[0]), it))
                    tern_i[0] += 1
            it += 1
    while pend_tail:
        e_rx = tail_part1(pend_tail[0])
        tail_part2(pend_tail.pop(0), e_rx)
    while pend_tern:
        c.e_tern = ternB(pend_tern.pop(0)[0])
    grp = [list(range(c.ncc))]
    anchors1 = (18, 24, 30, 36)
    anchors2 = (40, 45, 50, 55)
    e_cc = None
    for pidx in range(4):
        a = (c.e_o_list[anchors1[pidx]]
             if len(c.e_o_list) > anchors1[pidx] else None)
        outsl = c.gout1[pidx * 2048: pidx * 2048 + c.ncc * 256, :]
        e_cc = p.g(lambda e, p_=pidx, o_=outsl: e.collective_compute(
            "AllGather", ALU.bypass, replica_groups=grp,
            ins=[c.gin1[2 * p_:2 * p_ + 2].opt()], outs=[o_.opt()]),
            after=[a, e_cc, c.e_tern] +
                  c.tern_done.get(2 * pidx, []) +
                  c.tern_done.get(2 * pidx + 1, []))
    for pidx in range(4):
        a = (c.e_o_list[anchors2[pidx]]
             if len(c.e_o_list) > anchors2[pidx] else None)
        outsl = c.gout2[pidx * 4096: pidx * 4096 + c.ncc * 512, :]
        e_cc = p.g(lambda e, p_=pidx, o_=outsl: e.collective_compute(
            "AllGather", ALU.bypass, replica_groups=grp,
            ins=[c.gin2[4 * p_:4 * p_ + 4].opt()], outs=[o_.opt()]),
            after=[a, e_cc] + list(c.tern_spills.values()))
    c.e_cc = e_cc
    c.e_p3_banks = {k: st.get(("bank", k)) for k in range(8)}
    c.e_p3_banks[6] = st.get(("bank", 6))
    c.e_p3_banks[7] = c.b7["last"]
    if "p3" in c.dbg:
        _dbg_out(c, "dbg_O", c.O_sb, [128, 32768], BF16, [c.e_attn])
        _dbg_out(c, "dbg_w1blk", c.gout1[2 * 2048 + 128:2 * 2048 + 256, :],
                 [128, 2048], BF16, [c.e_cc])
        _dbg_out(c, "dbg_w2blk", c.gout2[4096 + 3 * 128:4096 + 4 * 128, :],
                 [128, 1024], BF16, [c.e_cc])


def _p4(c):
    p, v = c.p, c.view
    wo_col = [v(c.pool, 44032 + i * 4096, 128, 2048, BF16) for i in range(2)]
    ev_sb = v(c.pool, 24576, 128, 2048, F32)
    stg4 = [v(c.pool, 32768 + i * 2048, 128, 512, F32) for i in range(4)]
    st = {}
    e_last_spill = None
    for dc in range(DC):
        bi = dc % 2
        src = c.wo[:, dc * 128:(dc + 1) * 128].rearrange(
            "(hc p) j -> p hc j", p=128)
        e_w = p.gdma(lambda e, s_=src, b=bi: e.dma_start(
            wo_col[b].rearrange("p (hc j) -> p hc j", j=128), s_),
            after=[st.get(("w", bi))] +
                  ([c.e_tern] if dc < 2 else []), slot=f"wo{bi}")
        last = {}
        for hcc in range(16):
            for j in range(4):
                aft = []
                if hcc == 0:
                    aft = [e_w, st.get(("bank", j), c.e_p3_banks.get(j))]
                    if dc == 0:
                        aft += [c.e_attn]
                last[j] = p.t(lambda e, h_=hcc, jj=j, b=bi: e.matmul(
                    c.bank(jj), wo_col[b][:, h_ * 128:h_ * 128 + 128],
                    c.O_sb[:, h_ * 2048 + jj * 512: h_ * 2048 + jj * 512 + 512],
                    start=(h_ == 0), stop=(h_ == 15), skip_group_check=True),
                    after=aft)
        st[("w", bi)] = last[3]
        e_evs = {}
        for j in range(4):
            e_evs[j] = p.s(lambda e, jj=j: e.activation(
                ev_sb[:, jj * 512:(jj + 1) * 512], c.bank(jj), AF.Identity,
                scale=1.0), after=[last[j], st.get("evsb_user")])
            st[("bank", j)] = e_evs[j]
        tp_last = {}
        for tt in range(16):
            b4 = 4 + tt // 4
            aft = [e_evs[tt // 4]]
            if tt % 4 == 0:
                aft += [st.get(("bank", b4), c.e_p3_banks.get(b4)), c.e_idf]
            tp_last[b4] = p.t(lambda e, t_=tt, b_=b4: e.transpose(
                c.bank(b_)[:, (t_ % 4) * 128:(t_ % 4) * 128 + 128],
                ev_sb[:, t_ * 128:t_ * 128 + 128], c.sb_idf), after=aft)
        st["evsb_user"] = tp_last[7]
        for gi in range(4):
            e_cp = p.v(lambda e, g_=gi: e.tensor_copy(
                stg4[g_], c.bank(4 + g_)),
                after=[tp_last[4 + gi], st.get(("stg", gi))] +
                      ([c.e_attn, c.e_tern] if dc == 0 else []))
            st[("bank", 4 + gi)] = e_cp
            for u in range(4):
                tt = gi * 4 + u
                e_last_spill = p.sydma(
                    lambda e, t_=tt, d_=dc, g_=gi, u_=u: e.dma_start(
                        c.o1_scr[t_, :, d_ * 128:(d_ + 1) * 128],
                        stg4[g_][:, u_ * 128:u_ * 128 + 128]),
                    after=[e_cp] if u == 0 else [], slot=f"st4_{gi}")
            st[("stg", gi)] = e_last_spill
    c.e_p4 = e_last_spill
    c.e_p4_allspills = [st[("stg", gi)] for gi in range(4)]
    c.e_p4_banks = {k: st.get(("bank", k)) for k in range(8)}
    if "p4" in c.dbg:
        _dbg_out(c, "dbg_o1", c.o1_scr[3], [128, 2048], F32,
                 c.e_p4_allspills)
        _dbg_out(c, "dbg_evsb", ev_sb, [128, 2048], F32, c.e_p4_allspills)
        _dbg_out(c, "dbg_O2", c.O_sb, [128, 32768], BF16, c.e_p4_allspills)


def _p56(c):
    p, v = c.p, c.view
    o1b = v(c.pool, 16384, 128, 2048, F32)
    xb = v(c.pool, 24576, 128, 2048, F32)
    t0b = [v(c.pool, 32768, 128, 2048, F32), v(c.pool, 0, 128, 2048, F32)]
    ln2ob = [v(c.pool, 40960, 128, 2048, F32),
             v(c.pool, 8192, 128, 2048, F32)]
    xn3s = [v(c.pool, 49152, 128, 2048, BF16), v(c.pool, 61440, 128, 2048, BF16)]
    scr_s = v(c.pool, 53248, 128, 2048, BF16)
    scr_v = v(c.pool, 57344, 128, 2048, BF16)
    xb2 = v(c.bigb, 0, 128, 2048, F32)
    c.xn3T = v(c.biga, 0, 128, 32768, BF16)
    xnT3 = c.xn3T.rearrange("p (dcc t) -> p dcc t", t=2048)
    st = {}
    prev_evac = None
    pend = []

    def emit_tp(tt, e_ap3):
        nonlocal prev_evac

        def dst_fn(gi, t_=tt):
            return xnT3[:, gi * 8:gi * 8 + 8, t_ * 128:t_ * 128 + 128]
        aft_t = [e_ap3, prev_evac]
        if tt == 0:
            aft_t += [c.e_p4_banks.get(6), c.e_p4_banks.get(7)]
        evs = _transpose16(c, xn3s[tt % 2], dst_fn, BF16, aft_t, banks=(6, 7))
        prev_evac = evs[-1]

    def stageA(tt):
        pr = tt % 2
        t0 = t0b[pr]
        e_o1 = p.sydma(lambda e, t_=tt: e.dma_start(
            o1b, c.o1_scr[t_]),
            after=[c.e_p4 if tt == 0 else None, st.get("o1_user")] +
                  ((c.e_p4_allspills + list(c.tern_spills.values()) +
                    [c.e_cc]) if tt == 0 else []), slot="o1b")
        e_t0 = p.v(lambda e, t0_=t0: e.tensor_tensor(
            t0_, o1b, c.sb_wob_bc, ALU.add),
            after=[e_o1, c.e_wob, st.get(("t0_user_a", pr)),
                   st.get(("t0_user_b", pr))])
        st["o1_user"] = e_t0
        s2 = c.sm[2 + pr]
        e_sum = p.s(lambda e, t0_=t0, s_=s2: e.activation(
            scr_s, t0_, AF.Identity, accum_out=s_.sum), after=[e_t0])
        e_sq = p.v(lambda e, t0_=t0, s_=s2: e.scalar_tensor_tensor(
            scr_v, t0_, 1.0, t0_, ALU.mult, ALU.mult, accum_out=s_.sq),
            after=[e_t0])
        e_rstd, e_bias = _ln_small_chain(c, s2, 1.0 / D, LN_EPS, e_sum, e_sq)
        return (tt, e_rstd, e_bias)

    def stageB(item):
        tt, e_rstd2, e_bias2 = item
        pr = tt % 2
        t0 = t0b[pr]
        ln2o = ln2ob[pr]
        xbb = xb if pr == 0 else xb2
        s2 = c.sm[2 + pr]
        e_xb = p.sydma(lambda e, t_=tt, x_=xbb: e.dma_start(
            x_, c.x[t_ * 128:(t_ + 1) * 128, :]),
            after=[st.get(("xb_user", pr))] +
                  ([c.e_p4] if tt <= 1 else []), slot=f"xb5{pr}")
        e_ap2 = p.s(lambda e, t0_=t0, l_=ln2o, s_=s2: e.activation(
            l_, t0_, AF.Identity, bias=s_.bias, scale=s_.rstd),
            after=[e_bias2, e_rstd2, st.get(("ln2o_user", pr))])
        e_x2 = p.v(lambda e, t0_=t0, l_=ln2o, x_=xbb: e.tensor_tensor(
            t0_, l_, x_, ALU.add), after=[e_ap2, e_xb])
        st[("xb_user", pr)] = e_x2
        st[("ln2o_user", pr)] = e_x2
        e_sp = p.sydma(lambda e, t_=tt, t0_=t0: e.dma_start(
            c.x2_scr[t_], t0_), after=[e_x2], slot=f"x2sp{pr}")
        e_ap3 = _ln_stats_apply(c, t0, xn3s[pr], 1.0 / D, LN_EPS,
                                pr, [e_x2], scr_s, scr_v)
        st[("t0_user_a", pr)] = e_ap3
        st[("t0_user_b", pr)] = e_sp
        pend.append((tt, e_ap3))
        if len(pend) > 1:
            emit_tp(*pend.pop(0))

    pendA = []
    for tt in range(NTT):
        pendA.append(stageA(tt))
        if len(pendA) > 1:
            stageB(pendA.pop(0))
    while pendA:
        stageB(pendA.pop(0))
    while pend:
        emit_tp(*pend.pop(0))
    c.e_p56 = prev_evac
    c.e_x2_spill = st[("t0_user_b", 1)]
    c.e_x2_spill2 = st[("t0_user_b", 0)]
    if "p56" in c.dbg:
        _dbg_out(c, "dbg_xn3T", c.xn3T, [128, 32768], BF16, [c.e_p56])
        _dbg_out(c, "dbg_x2", c.x2_scr[6], [128, 2048], F32,
                 [c.e_x2_spill, c.e_p56])


def _p8(c):
    p, v = c.p, c.view
    w1sb = [v(c.pool, 16384 + i * 4096, 128, 2048, BF16) for i in range(3)]
    w2h = [v(c.pool, 57472 + i * 2048, 128, 1024, BF16) for i in range(3)]
    jyst = v(c.pool, 0, 128, 4096, BF16)
    rl = v(c.pool, 28672, 128, 512, BF16)
    h2 = v(c.pool, 29696, 128, 512, BF16)
    jy = v(c.pool, 30720, 128, 1024, BF16)
    junk2 = v(c.pool, 32768, 128, 1024, BF16)
    x2b = v(c.pool, 34816, 128, 2048, F32)
    outb = v(c.pool, 43008, 128, 2048, F32)
    srow2 = v(c.pool, 51200, 1, 512, F32)
    rrow2 = v(c.pool, 53248, 1, 512, F32)
    rstd_t = v(c.pool, 55296, 128, 4, F32)
    cs_t = v(c.pool, 55360, 128, 1, F32)
    smacc = []
    off = 55424
    for i in range(4):
        s8 = Ctx()
        for nm in ("s0", "s1", "q0", "q1"):
            setattr(s8, nm, v(c.pool, off, 128, 1, F32))
            off += 64
        smacc.append(s8)
    smch = []
    for i in range(2):
        s8 = Ctx()
        for nm in ("sum", "sq", "mean", "tmp", "var", "log", "rstd", "bias"):
            setattr(s8, nm, v(c.pool, off, 128, 1, F32))
            off += 64
        smch.append(s8)
    assert off <= 57472

    h_sb = v(c.bigb, 0, 128, 32768, BF16)
    jy1 = {0: v(c.bigb, 63488, 128, 1024, BF16),
           1: v(c.bigb, 61440, 128, 1024, BF16)}
    jy2 = [jy, v(c.bigb, 59392, 128, 1024, BF16)]
    st = {("fc", 0): c.e_p4_banks.get(0), ("fc", 1): c.e_p4_banks.get(1),
          "ms": c.e_p4_banks.get(2), "rt": c.e_p4_banks.get(3)}
    pa_user = {j: c.e_p4_banks.get(4 + j) for j in range(4)}
    pa_user[0] = [c.e_p4_banks.get(0), c.e_p4_banks.get(1)]
    pa_user = {0: [c.e_p4_banks.get(0), c.e_p4_banks.get(1)],
               1: [c.e_p4_banks.get(2), c.e_p4_banks.get(3)],
               2: [c.e_p4_banks.get(4), c.e_p4_banks.get(5)],
               3: [c.e_p4_banks.get(6), c.e_p4_banks.get(7)]}
    h_guard = [c.e_p4, c.e_p56]
    prev_w1_user = [None, None, None]
    prev_w2_user = [None, None, None]
    ln4_pend = []
    for s_ in range(NSTRIP):
        e_h_prev = None
        e_summ = None
        e_rl = None
        h2ev = None
        for hcc in range(HC):
            bi = hcc % 3
            pb = hcc % 2
            aftw = [prev_w1_user[bi]]
            if s_ == 0 and hcc < 3:
                aftw += [c.e_cc, c.e_p4, c.e_p56]
            w1row = ((hcc % 8) // 2) * 2048 + (hcc // 8) * 256 + (hcc % 2) * 128
            e_w = p.sydma(lambda e, r_=w1row, b=bi: e.dma_start(
                w1sb[b], c.gout1[r_:r_ + 128, :]), after=aftw,
                slot=f"w1_{bi}")
            last_mm = None
            for dcc in range(DC):
                aft = []
                if dcc == 0:
                    aft = [e_w, st[("fc", pb)]]
                    if hcc <= 1:
                        aft += [c.e_p56 if s_ == 0 else None]
                        aft += pa_user[0] if s_ > 0 or True else []
                last_mm = p.t(lambda e, d_=dcc, h_=hcc, b=bi, pb_=pb, ss=s_:
                              e.matmul(
                    c.bank(pb_), w1sb[b][:, d_ * 128:d_ * 128 + 128],
                    c.xn3T[:, d_ * 2048 + ss * 512: d_ * 2048 + ss * 512 + 512],
                    start=(d_ == 0), stop=(d_ == DC - 1),
                    skip_group_check=True), after=aft)
            prev_w1_user[bi] = last_mm
            if hcc > 0:
                aftsum = [h2ev, c.e_ones]
                if hcc == 1:
                    aftsum += [st.get("ms")] + pa_user[1]
                e_summ = p.t(lambda e, h_=hcc - 1: e.matmul(
                    c.bank(2)[0:1, :], c.sb_ones_col, h2,
                    start=(h_ == 0), stop=(h_ == HC - 1),
                    skip_group_check=True), after=aftsum)
            e_rl = p.s(lambda e, pb_=pb: e.activation(rl, c.bank(pb_), AF.Relu),
                       after=[last_mm, e_h_prev])
            st[("fc", pb)] = e_rl
            afth = [e_rl] + (h_guard if hcc == 0 else [])
            if hcc == 58:
                afth.append(st.get("jy2b_user"))
            elif hcc == 60:
                afth.append(st.get("jy1b_user"))
            elif hcc == 62:
                afth.append(st.get("jy1a_user"))
            e_h = p.v(lambda e, h_=hcc: e.tensor_tensor(
                h_sb[:, h_ * 512:(h_ + 1) * 512], rl, rl, ALU.mult),
                after=afth)
            e_h2 = p.v(lambda e, h_=hcc: e.tensor_tensor(
                h2, h_sb[:, h_ * 512:(h_ + 1) * 512],
                h_sb[:, h_ * 512:(h_ + 1) * 512], ALU.mult),
                after=[e_h, e_summ])
            e_h_prev = e_h2
            h2ev = e_h2
            if hcc in (6, 12, 18, 24) and ln4_pend:
                ln4_pend.pop(0)()
        e_summ = p.t(lambda e: e.matmul(
            c.bank(2)[0:1, :], c.sb_ones_col, h2,
            start=False, stop=True, skip_group_check=True),
            after=[h2ev, c.e_ones])
        if s_ == 0 and "p8" in c.dbg:
            _dbg_out(c, "dbg_h", h_sb, [128, 32768], BF16, [e_h_prev])
        e_sr = p.v(lambda e: e.tensor_copy(srow2, c.bank(2)[0:1, :]),
                   after=[e_summ, st.get("srow2_user")])
        st["ms"] = e_sr
        e_ms = p.v(lambda e: e.tensor_scalar(srow2, srow2, 1.0 / HID, RMS_EPS,
                                             ALU.mult, ALU.add), after=[e_sr])
        e_ln = p.s(lambda e: e.activation(rrow2, srow2, AF.Ln), after=[e_ms])
        e_rs = p.s(lambda e: e.activation(srow2, rrow2, AF.Exp, scale=-0.5),
                   after=[e_ln])
        st["srow2_user"] = e_rs
        tp = None
        for tt in range(4):
            aft = [e_rs, c.e_idf]
            if tt == 0:
                aft += [st.get("rt")]
            tp = p.t(lambda e, t_=tt: e.transpose(
                c.bank(3)[:, t_:t_ + 1], srow2[0:1, t_ * 128:t_ * 128 + 128],
                c.sb_idf[0:1, 0:1]), after=aft)
        e_rt = p.v(lambda e: e.tensor_copy(rstd_t, c.bank(3)[:, 0:4]),
                   after=[tp, st.get("rt_user")])
        st["rt"] = e_rt
        if s_ == 0 and "p8" in c.dbg:
            _dbg_out(c, "dbg_rstd", rstd_t, [128, 4], F32, [e_rt])
            _dbg_out(c, "dbg_ms", srow2, [1, 512], F32, [e_rs])
        e_half_guard = [st[("fc", 0)], st[("fc", 1)], st.get("ms"),
                        st.get("rt"), e_h_prev] + pa_user[2] + pa_user[3]
        e_ev0 = {}
        e_sq0 = {}
        for dmh in range(2):
            last_mm = None
            for hcc in range(HC):
                bi = hcc % 3
                aftw = [prev_w2_user[bi]]
                if s_ == 0 and dmh == 0 and hcc < 3:
                    aftw += [c.e_cc, c.e_p4, c.e_p56]
                qd_, jj_ = hcc // 16, hcc % 16
                w2row = ((jj_ // 4) * 4096 + (2 * qd_ + dmh) * 512 +
                         (jj_ % 4) * 128)
                e_w2 = p.gdma(lambda e, b=bi, r_=w2row: e.dma_start(
                    w2h[b], c.gout2[r_:r_ + 128, :]),
                    after=aftw, slot=f"w2_{bi}")
                for tt in range(4):
                    for cc2 in range(2):
                        aft = []
                        if tt == 0 and cc2 == 0:
                            aft = [e_w2]
                        if hcc == 0 and cc2 == 0:
                            if dmh == 0:
                                if tt == 0:
                                    aft += e_half_guard
                            else:
                                aft += [e_ev0[tt]]
                        last_mm = p.t(lambda e, h_=hcc, t_=tt, c2=cc2, b=bi:
                                      e.matmul(
                            c.pa[t_][:, c2 * 512:c2 * 512 + 512],
                            h_sb[:, h_ * 512 + t_ * 128:
                                 h_ * 512 + t_ * 128 + 128],
                            w2h[b][:, c2 * 512:c2 * 512 + 512],
                            start=(h_ == 0), stop=(h_ == HC - 1),
                            skip_group_check=True), after=aft)
                prev_w2_user[bi] = last_mm
            if dmh == 0:
                for tt in range(4):
                    acc = smacc[tt]
                    e_ev = p.s(lambda e, t_=tt, a_=acc: e.activation(
                        jyst[:, t_ * 1024:t_ * 1024 + 1024], c.pa[t_][:],
                        AF.Identity, scale=rstd_t[:, t_:t_ + 1],
                        accum_out=a_.s0),
                        after=[last_mm, e_rt, st.get(("jyu", tt))] +
                              ([c.e_p4, c.e_p56, c.e_x2_spill,
                                c.e_x2_spill2]
                               if (s_ == 0 and tt == 0) else []))
                    e_sq = p.v(lambda e, t_=tt, a_=acc: e.scalar_tensor_tensor(
                        junk2, jyst[:, t_ * 1024:t_ * 1024 + 1024], 1.0,
                        jyst[:, t_ * 1024:t_ * 1024 + 1024], ALU.mult,
                        ALU.mult, accum_out=a_.q0),
                        after=[e_ev, st.get("junk_w"), e_h_prev] +
                              ([c.e_x2_spill, c.e_x2_spill2]
                               if (s_ == 0 and tt == 0) else []))
                    st["junk_w"] = e_sq
                    e_ev0[tt] = e_ev
                    e_sq0[tt] = e_sq
                continue
            e_ev1s = {}
            for tt in range(4):
                jbuf = jy1[tt] if tt < 2 else jy2[tt - 2]
                e_ev1s[tt] = p.s(lambda e, t_=tt, a_=smacc[tt], j_=jbuf:
                                 e.activation(
                    j_, c.pa[t_][:], AF.Identity,
                    scale=rstd_t[:, t_:t_ + 1], accum_out=a_.s1),
                    after=[last_mm, e_rt,
                           st.get(("jy2u", tt)) if tt >= 2 else None])
            pa_user[0] = [e_ev1s[0]]
            pa_user[1] = [e_ev1s[1]]

            def mk_chain(tt, s__, e_ev1, e_ev0t, e_sq0t, e_rt_):
                def emit():
                    acc = smacc[tt]
                    smm = smch[tt % 2]
                    gtt = s__ * 4 + tt
                    e_x2 = p.gdma(lambda e, g_=gtt: e.dma_start(
                        x2b, c.x2_scr[g_]),
                        after=([c.e_x2_spill, c.e_x2_spill2, c.e_p56]
                               if (s__ == 0 and tt == 0) else []) +
                              [st.get("x2_user")], slot="x2b")
                    jbuf = jy1[tt] if tt < 2 else jy2[tt - 2]
                    e_sq1 = p.v(lambda e, a_=acc, j_=jbuf:
                                e.scalar_tensor_tensor(
                        junk2, j_, 1.0, j_, ALU.mult, ALU.mult,
                        accum_out=a_.q1),
                        after=[e_ev1, st.get("junk_w")])
                    st["junk_w"] = e_sq1
                    if tt >= 2:
                        st[("jy2u", tt)] = e_sq1
                    if tt == 3:
                        st["jy2b_user"] = e_sq1
                    es1 = p.v(lambda e, m_=smm, a_=acc: e.tensor_tensor(
                        m_.sum, a_.s0, a_.s1, ALU.add),
                        after=[e_ev0t, e_ev1, st.get(("sm", tt % 2))])
                    es2 = p.v(lambda e, m_=smm, a_=acc: e.tensor_tensor(
                        m_.sq, a_.q0, a_.q1, ALU.add),
                        after=[e_sq0t, e_sq1])
                    e_rstd, e_bias = _ln_small_chain(c, smm, 1.0 / D, LN_EPS,
                                                     es1, es2)
                    e_cs2 = None
                    if tt >= 2:
                        e_cs2 = p.v(lambda e, m_=smm, t_=tt: e.tensor_tensor(
                            cs_t, rstd_t[:, t_:t_ + 1], m_.rstd, ALU.mult),
                            after=[e_rstd, st.get("cs_user"), e_rt_])
                        st["rt_user"] = e_cs2
                    e_xp = p.v(lambda e, m_=smm: e.tensor_scalar(
                        x2b, x2b, m_.bias, None, ALU.add),
                        after=[e_x2, e_bias])
                    e_oa = p.v(lambda e, t_=tt, m_=smm: e.scalar_tensor_tensor(
                        outb[:, 0:1024], jyst[:, t_ * 1024:t_ * 1024 + 1024],
                        m_.rstd, x2b[:, 0:1024], ALU.mult, ALU.add),
                        after=[e_xp, e_rstd, st.get("outb_user"), e_sq0t])
                    if tt < 2:
                        e_ob = p.v(lambda e, m_=smm, j_=jbuf:
                                   e.scalar_tensor_tensor(
                            outb[:, 1024:2048], j_, m_.rstd,
                            x2b[:, 1024:2048], ALU.mult, ALU.add),
                            after=[e_oa, e_ev1])
                        st["jy1a_user" if tt == 0 else "jy1b_user"] = e_ob
                    else:
                        e_ob = p.v(lambda e, t_=tt: e.scalar_tensor_tensor(
                            outb[:, 1024:2048], c.pa[t_][:], cs_t,
                            x2b[:, 1024:2048], ALU.mult, ALU.add),
                            after=[e_oa, e_cs2])
                        pa_user[tt] = [e_ob]
                    st[("sm", tt % 2)] = e_ob
                    st["x2_user"] = e_ob
                    if tt >= 2:
                        st["cs_user"] = e_ob
                    st[("jyu", tt)] = e_oa
                    e_od = p.gdma(lambda e, g_=gtt: e.dma_start(
                        c.out[g_ * 128:(g_ + 1) * 128, :], outb),
                        after=[e_ob], slot="outd")
                    st["outb_user"] = e_od
                    st["e_out_last"] = e_od
                return emit

            for tt in range(4):
                ln4_pend.append(mk_chain(tt, s_, e_ev1s[tt], e_ev0[tt],
                                         e_sq0[tt], e_rt))
        h_guard = [prev_w2_user[0], prev_w2_user[1], prev_w2_user[2]]
        st[("fc", 0)] = pa_user[0][0]
        st[("fc", 1)] = pa_user[0][0]
        st["ms"] = pa_user[1][0]
        st["rt"] = pa_user[1][0]
    while ln4_pend:
        ln4_pend.pop(0)()
    c.e_final = st["e_out_last"]


def _finish(c):
    nc, p = c.nc, c.p
    tail = []
    for eng in ENGS:
        last_by_slot = {}
        for r in p.recs[eng]:
            if r.dma:
                last_by_slot[r.slot] = r.ev
        tail.extend(last_by_slot.values())
    if tail:
        p.add("sy", lambda e: e.nop(), after=tail, dma=False)
    p.check()
    p.resolve()
    c.sems = {}
    for eng in ENGS:
        c.sems[("c", eng)] = c.stack.enter_context(nc.semaphore(f"c_{eng}"))
    for eng in ENGS:
        slots = sorted({r.slot for r in p.recs[eng] if r.dma})
        for sl_ in slots:
            c.sems[("d", eng, sl_)] = c.stack.enter_context(
                nc.semaphore(f"d_{eng}_{sl_}"))
    with nc.Block() as block:
        @block.tensor
        def _(t):
            p.emit("t", t, c.sems)

        @block.scalar
        def _(s):
            p.emit("s", s, c.sems)

        @block.vector
        def _(vv):
            p.emit("v", vv, c.sems)

        @block.gpsimd
        def _(g):
            p.emit("g", g, c.sems)

        @block.sync
        def _(sy):
            p.emit("sy", sy, c.sems)
    c.stack.close()


def _host_inputs(wq_a, b_qa, wq_b, b_qb, wkv_a, b_kva, wkv_b, b_kvb, wo, b_wo,
                 wfc, wproj):
    qb_perm, kvb_perm, kva_perm = _permutations()
    return {
        "wq_a": np.ascontiguousarray(np.asarray(wq_a)),
        "b_qa": np.ascontiguousarray(np.asarray(b_qa)),
        "wq_b": np.ascontiguousarray(np.asarray(wq_b)[:, qb_perm]),
        "b_qb": np.ascontiguousarray(np.asarray(b_qb)[qb_perm]),
        "wkv_a": np.ascontiguousarray(np.asarray(wkv_a)[:, kva_perm]),
        "b_kva": np.ascontiguousarray(np.asarray(b_kva)[kva_perm]),
        "wkv_b": np.ascontiguousarray(np.asarray(wkv_b)[:, kvb_perm]),
        "b_kvb": np.ascontiguousarray(np.asarray(b_kvb)[kvb_perm]),
        "wo": np.ascontiguousarray(np.asarray(wo)),
        "b_wo": np.ascontiguousarray(np.asarray(b_wo)),

        "id_f32": np.eye(128, dtype=np.float32),
        "id_bf": np.eye(128, dtype=np.float32),
        "ones_col": np.ones((128, 1), dtype=np.float32),
        "ones_row": np.ones((1, 128), dtype=np.float32),
        "rep32": np.equal(np.arange(128)[None, :] % 32,
                          np.arange(32)[:, None]).astype(np.float32),
    }


def _w_slices(wfc, wproj, core):
    wfc = np.asarray(wfc)
    wproj = np.asarray(wproj)
    wsl1 = np.ascontiguousarray(wfc[core * 1024:(core + 1) * 1024, :])
    qd, half = core // 2, core % 2
    parts = [wproj[(8 * half + k) * 128:(8 * half + k + 1) * 128,
                   qd * 2048:(qd + 1) * 2048] for k in range(8)]
    wsl2 = np.ascontiguousarray(np.concatenate(parts, axis=0))
    return wsl1, wsl2


def kernel(L0_attn_x, L0_attn_wq_a_w, L0_attn_wq_a_b, L0_attn_wq_b_w,
           L0_attn_wq_b_b, L0_attn_wkv_a_w, L0_attn_wkv_a_b, L0_attn_wkv_b_w,
           L0_attn_wkv_b_b, L0_attn_wo_w, L0_attn_wo_b, L0_ffn_W_fc,
           L0_ffn_W_proj):
    common = _host_inputs(
        L0_attn_wq_a_w, L0_attn_wq_a_b, L0_attn_wq_b_w, L0_attn_wq_b_b,
        L0_attn_wkv_a_w, L0_attn_wkv_a_b, L0_attn_wkv_b_w, L0_attn_wkv_b_b,
        L0_attn_wo_w, L0_attn_wo_b, L0_ffn_W_fc, L0_ffn_W_proj)
    nc, _ = build_program()
    xs = np.asarray(L0_attn_x)
    in_maps = []
    for b_ in range(B):
        w1s, w2s = _w_slices(L0_ffn_W_fc, L0_ffn_W_proj, b_)
        in_maps.append(dict(common, x=np.ascontiguousarray(xs[b_]),
                            wsl1=w1s, wsl2=w2s))
    res = run_bass_kernel_spmd(nc, in_maps, core_ids=list(range(B)))
    return np.stack([r["out"] for r in res.results], axis=0)
